# version 16
# baseline (speedup 1.0000x reference)
import math
import numpy as np
import ml_dtypes
import concourse.bass as bass
import concourse.mybir as mybir
from concourse.bass_utils import run_bass_kernel_spmd
from concourse.aot_env import allow_ici_all_to_all

F32 = mybir.dt.float32
BF16 = mybir.dt.bfloat16
U8 = mybir.dt.uint8
AF = mybir.ActivationFunctionType
ALU = mybir.AluOpType
AX = mybir.AxisListType

S = 8192
DM = 4096
NH, KVH, D = 32, 8, 128
I_FULL = 11008
I_PAD = 11264
ITC = I_PAD // 8
W = 8
TOK = S // W
HPC = NH // W
EXP_SHIFT = -20.0
EPS = 1e-5
N_ITILES = I_FULL // 128
CHUNK_COLS = [256, 256, 256, 256, 256, 128]
SB_TILES = []
for _k in range(6):
    _tiles = []
    for _dl in range(CHUNK_COLS[_k] // 128):
        _l = 2 * _k + _dl
        for _r in range(8):
            if _r * 11 + _l < 86:
                _tiles.append((_r, _dl))
    SB_TILES.append(_tiles)

NF4 = np.array([-1.0, -0.6961928009986877, -0.5250730514526367, -0.39491748809814453,
                -0.28444138169288635, -0.18477343022823334, -0.09105003625154495, 0.0,
                0.07958029955625534, 0.16093020141124725, 0.24611230194568634,
                0.33791524171829224, 0.44070982933044434, 0.5626170039176941,
                0.7229568362236023, 1.0], dtype=np.float32)

DT_BYTES = {F32: 4, BF16: 2, U8: 1}

DBG_TAGS = {"x2": "out", "xln_own": "xln_own_a", "xln": "xln_all_a", "attn": "attn_loc_p0", "a2a_in": "attn_a2a_p0"}



def host_prep(inputs):
    x = np.ascontiguousarray(np.asarray(inputs["L0_attn_x"]).reshape(S, DM),
                             dtype=np.float32)

    def u8(a):
        return np.ascontiguousarray(np.asarray(a).astype(np.uint8))

    def exp_fp4(s):
        return np.repeat(np.ascontiguousarray(np.asarray(s), dtype=np.float32),
                         128, axis=0)

    qc, kc, vc = (u8(inputs["L0_attn_q_codes"]), u8(inputs["L0_attn_k_codes"]),
                  u8(inputs["L0_attn_v_codes"]))
    oc = u8(inputs["L0_attn_o_codes"])
    qs, ks, vs = (exp_fp4(inputs["L0_attn_q_scales"]),
                  exp_fp4(inputs["L0_attn_k_scales"]),
                  exp_fp4(inputs["L0_attn_v_scales"]))
    os_ = exp_fp4(inputs["L0_attn_o_scales"])

    def padi(a, axis):
        pad = [(0, 0), (0, 0)]
        pad[axis] = (0, I_PAD - I_FULL)
        return np.pad(a, pad)

    gT = padi(u8(np.asarray(inputs["L0_ffn_gate_codes"]).T), 1)
    uT = padi(u8(np.asarray(inputs["L0_ffn_up_codes"]).T), 1)
    dT = padi(u8(np.asarray(inputs["L0_ffn_down_codes"]).T), 0)
    gsT = padi(np.repeat(np.ascontiguousarray(
        np.asarray(inputs["L0_ffn_gate_scales"]).T, dtype=np.float32), 64, axis=0), 1)
    usT = padi(np.repeat(np.ascontiguousarray(
        np.asarray(inputs["L0_ffn_up_scales"]).T, dtype=np.float32), 64, axis=0), 1)
    dsT = padi(np.repeat(np.ascontiguousarray(
        np.asarray(inputs["L0_ffn_down_scales"]).T, dtype=np.float32), 64, axis=0), 0)

    half = D // 2
    inv_freq = (1.0 / (10000.0 ** (np.arange(half, dtype=np.float32) / half))).astype(np.float32)
    pos = np.arange(S, dtype=np.float32)
    ang = (inv_freq[:, None] * pos[None, :]).astype(np.float32)
    cosv = np.cos(ang).astype(np.float32)
    sinv = np.sin(ang).astype(np.float32)
    rs = np.float32(1.0 / math.sqrt(D))
    cs_packed = np.zeros((half, 16 * 2048), dtype=np.float32)
    for j in range(16):
        sl = slice(j * 512, (j + 1) * 512)
        cs_packed[:, j * 2048:j * 2048 + 512] = cosv[:, sl] * rs
        cs_packed[:, j * 2048 + 512:j * 2048 + 1024] = sinv[:, sl] * rs
        cs_packed[:, j * 2048 + 1024:j * 2048 + 1536] = cosv[:, sl]
        cs_packed[:, j * 2048 + 1536:j * 2048 + 2048] = sinv[:, sl]

    masks = np.zeros((128, 4 * 512), dtype=np.float32)
    pp = np.arange(128)[:, None]
    ff = np.arange(512)[None, :]
    for t in range(4):
        masks[:, t * 512:(t + 1) * 512] = np.where(128 * t + pp <= ff, 0.0, -1e30)

    ident = np.eye(128, dtype=np.float32).astype(ml_dtypes.bfloat16)

    in_maps = []
    for c in range(W):
        in_maps.append({
            "x_own": np.ascontiguousarray(x[c * TOK:(c + 1) * TOK]),
            "wq_c": np.ascontiguousarray(qc[:, c * 512:(c + 1) * 512]),
            "wq_s": np.ascontiguousarray(qs[:, c * 512:(c + 1) * 512]),
            "wk_c": np.ascontiguousarray(kc[:, c * 128:(c + 1) * 128]),
            "wk_s": np.ascontiguousarray(ks[:, c * 128:(c + 1) * 128]),
            "wv_c": np.ascontiguousarray(vc[:, c * 128:(c + 1) * 128]),
            "wv_s": np.ascontiguousarray(vs[:, c * 128:(c + 1) * 128]),
            "wo_c": oc, "wo_s": os_,
            "g_c": np.ascontiguousarray(gT[:, c * ITC:(c + 1) * ITC]),
            "g_s": np.ascontiguousarray(gsT[:, c * ITC:(c + 1) * ITC]),
            "u_c": np.ascontiguousarray(uT[:, c * ITC:(c + 1) * ITC]),
            "u_s": np.ascontiguousarray(usT[:, c * ITC:(c + 1) * ITC]),
            "d_c": np.ascontiguousarray(dT[c * ITC:(c + 1) * ITC, :]),
            "d_s": np.ascontiguousarray(dsT[c * ITC:(c + 1) * ITC, :]),
            "cs_tab": cs_packed, "masks": masks, "ident": ident,
        })
    return in_maps



class Builder:
    SEMS = ("c0", "x0", "x1", "st1", "p0", "p1", "p2", "p3", "p4", "p5",
            "w0", "w1", "g0", "g1", "as", "ad", "gp", "te", "ve", "sc", "cc")

    def __init__(self, nc, max_phase=6, debug=()):
        self.nc = nc
        self.max_phase = max_phase
        self.debug = set(debug)
        self.cur = None
        self.eng = None
        self.sems = {}
        self.dram = {}
        self.ps = {}

    def _reset(self):
        self.c = {s: 0 for s in self.SEMS}
        self.dbg_jobs = []

    def _inc(self, inst, name, n):
        if inst is not None:
            inst.then_inc(self.sems[name], n)
        self.c[name] += n
        return self.c[name]

    def wait(self, engname, name, val=None):
        v = self.c[name] if val is None else val
        if self.cur == engname and v > 0:
            self.eng.wait_ge(self.sems[name], int(v))
        return v

    def dma(self, engname, out, in_, sem, accum=None):
        inst = None
        if self.cur == engname:
            kw = {}
            if accum is not None:
                kw["accum_op"] = accum
            inst = self.eng.dma_start(out=out, in_=in_, **kw)
        return self._inc(inst, sem, 16)

    def op(self, engname, meth, *a, sem=None, n=1, **kw):
        inst = None
        if self.cur == engname:
            inst = getattr(self.eng, meth)(*a, **kw)
        if sem is not None:
            return self._inc(inst, sem, n)
        return None

    def coll(self, kind, ins, outs):
        inst = None
        if self.cur == "gpsimd":
            with allow_ici_all_to_all():
                inst = self.eng.collective_compute(
                    kind, ALU.bypass, replica_groups=[list(range(W))],
                    ins=ins, outs=outs)
        return self._inc(inst, "cc", 1)

    def alloc(self):
        nc = self.nc
        dd = self.dram
        io = [("x_own", [TOK, DM], F32, "ExternalInput"),
              ("wq_c", [DM, 512], U8, "ExternalInput"),
              ("wq_s", [DM, 512], F32, "ExternalInput"),
              ("wk_c", [DM, 128], U8, "ExternalInput"),
              ("wk_s", [DM, 128], F32, "ExternalInput"),
              ("wv_c", [DM, 128], U8, "ExternalInput"),
              ("wv_s", [DM, 128], F32, "ExternalInput"),
              ("wo_c", [DM, DM], U8, "ExternalInput"),
              ("wo_s", [DM, DM], F32, "ExternalInput"),
              ("g_c", [DM, ITC], U8, "ExternalInput"),
              ("g_s", [DM, ITC], F32, "ExternalInput"),
              ("u_c", [DM, ITC], U8, "ExternalInput"),
              ("u_s", [DM, ITC], F32, "ExternalInput"),
              ("d_c", [ITC, DM], U8, "ExternalInput"),
              ("d_s", [ITC, DM], F32, "ExternalInput"),
              ("cs_tab", [64, 16 * 2048], F32, "ExternalInput"),
              ("masks", [128, 4 * 512], F32, "ExternalInput"),
              ("ident", [128, 128], BF16, "ExternalInput"),
              ("out", [TOK, DM], F32, "ExternalOutput"),
              ("xln_own_a", [128, 32 * (TOK // 2)], BF16, "Internal"),
              ("xln_own_b", [128, 32 * (TOK // 2)], BF16, "Internal"),
              ("xln_all_a", [W * 128, 32 * (TOK // 2)], BF16, "Internal"),
              ("xln_all_b", [W * 128, 32 * (TOK // 2)], BF16, "Internal"),
              ("attn_a2a_p0", [W * 128, 2 * TOK], BF16, "Internal"),
              ("attn_a2a_p1", [W * 128, 2 * TOK], BF16, "Internal"),
              ("attn_loc_p0", [W * 128, 2 * TOK], BF16, "Internal"),
              ("attn_loc_p1", [W * 128, 2 * TOK], BF16, "Internal")]
        for k in range(6):
            cwk = CHUNK_COLS[k]
            dlk = cwk // 128
            io.append((f"gw_own{k}", [128, dlk * DM], BF16, "Internal"))
            io.append((f"uw_own{k}", [128, dlk * DM], BF16, "Internal"))
            io.append((f"gw_all{k}", [W * 128, dlk * DM], BF16, "Internal"))
            io.append((f"uw_all{k}", [W * 128, dlk * DM], BF16, "Internal"))
            io.append((f"dw_own{k}", [128, 32 * cwk], BF16, "Internal"))
            io.append((f"dw_all{k}", [W * 128, 32 * cwk], BF16, "Internal"))
        for nm, shp, dt_, kind in io:
            dd[nm] = nc.dram_tensor(nm, shp, dt_, kind=kind).ap()
        if "wq" in self.debug:
            dd["dbg_wq"] = nc.dram_tensor("dbg_wq", [128, 32 * 512], BF16,
                                          kind="ExternalOutput").ap()
        if "qkv" in self.debug:
            dd["dbg_q"] = nc.dram_tensor("dbg_q", [128, HPC * S], BF16,
                                         kind="ExternalOutput").ap()
            dd["dbg_k"] = nc.dram_tensor("dbg_k", [128, S], BF16,
                                         kind="ExternalOutput").ap()
            dd["dbg_v"] = nc.dram_tensor("dbg_v", [128, S], BF16,
                                         kind="ExternalOutput").ap()
        for tag in self.debug:
            if tag in ("qkv", "wq"):
                continue
            src = dd[DBG_TAGS[tag]]
            dd["dbg_" + tag] = nc.dram_tensor(
                "dbg_" + tag, list(src.shape), src.dtype,
                kind="ExternalOutput").ap()

        self.A = nc.alloc_sbuf_tensor("arA", [128, 65536], U8)
        self.B = nc.alloc_sbuf_tensor("arB", [128, 32768], U8)
        self.C = nc.alloc_sbuf_tensor("arC", [128, 32768], U8)
        self.Dr = nc.alloc_sbuf_tensor("arD", [128, 32768], U8)
        self.E = nc.alloc_sbuf_tensor("arE", [128, 16384], U8)
        self.F = nc.alloc_sbuf_tensor("arF", [128, 20480], U8)
        self.G = nc.alloc_sbuf_tensor("arG", [128, 8192], U8)
        self.ones_bf = nc.alloc_sbuf_tensor("ones_bf", [128, 128], BF16)
        self.consts = nc.alloc_sbuf_tensor("consts", [128, 2], F32)
        self.ident_sb = nc.alloc_sbuf_tensor("ident_sb", [128, 128], BF16)
        self.stats = nc.alloc_sbuf_tensor("stats", [128, 64], F32)
        for i in range(8):
            self.ps[i] = nc.alloc_psum_tensor(f"pb{i}", [128, 512], F32)

    def v(self, arena, off, dtype, cols, parts=128, p0=0):
        nb = cols * DT_BYTES[dtype]
        return arena.ap()[p0:p0 + parts, off:off + nb].bitcast(dtype)

    def psum(self, i, cols=512, dtype=F32):
        if dtype == F32:
            return self.ps[i].ap()[:, 0:cols]
        return self.ps[i].ap().bitcast(dtype)[:, 0:cols]

    def dbg(self, tag):
        if tag in self.debug:
            self.dbg_jobs.append(tag)

    def build(self):
        self.alloc()
        import contextlib
        stack = contextlib.ExitStack()
        for s in self.SEMS:
            self.sems[s] = stack.enter_context(self.nc.semaphore(s))
        with stack:
            with self.nc.Block() as block:
                for name, starter in (("sync", block.sync),
                                      ("gpsimd", block.gpsimd),
                                      ("scalar", block.scalar),
                                      ("vector", block.vector),
                                      ("tensor", block.tensor)):
                    def run(e, name=name):
                        self.cur = name
                        self.eng = e
                        self._reset()
                        self.schedule()
                    starter(run)

    def schedule(self):
        mp = self.max_phase
        self.ph_init()
        self._ph2_idx = 0
        self._ph2_state = {"map": self.ph2_map("ph1")}
        self._ph2_maps = {}
        self._ph2_barriers = {}
        self.ag_cc = {}
        self._ph2g = self.ph2_gen() if mp >= 2 else None
        self._ph0g = self.ph0_gen()
        with self.nc.named_scope("ph1"):
            if mp >= 1:
                self.ph1()
        with self.nc.named_scope("ph0"):
            self.ph0()
        assert self._ph2g is None or self._ph2_idx == self.PH2_N1 \
            or mp < 6, self._ph2_idx
        bar3 = [("st1", self.c["st1"]), ("ve", self.c["ve"])]
        self._ph2_maps[self.PH2_N1] = self.ph2_map("ph3")
        self._ph2_barriers[self.PH2_N1] = {
            eng: list(bar3) for eng in ("sync", "scalar", "vector")}
        self.mark_ph1 = dict(gp=self.c["gp"], sc=self.c["sc"],
                             w0=self.c["w0"], w1=self.c["w1"])
        with self.nc.named_scope("ph3"):
            if mp >= 3:
                self.ph3()
        assert self._ph2g is None or self._ph2_idx == self.PH2_N3 \
            or mp < 6, self._ph2_idx
        tebar = self.c["te"]
        self._ph2_maps[self.PH2_N3] = self.ph2_map("ph4")
        self._ph2_barriers[self.PH2_N3] = {
            eng: [("te", tebar)]
            for eng in ("sync", "scalar", "gpsimd", "vector")}
        with self.nc.named_scope("ph4"):
            if mp >= 4:
                self.ph4()
        assert self._ph2g is None or self._ph2_idx == self.PH2_N4 \
            or mp < 6, self._ph2_idx
        bar5 = [("te", self.c["te"]), ("ve", self.c["ve"]),
                ("as", self.c["as"])]
        self._ph2_maps[self.PH2_N4] = self.ph2_map("ph5")
        self._ph2_barriers[self.PH2_N4] = {
            eng: list(bar5) for eng in ("sync", "scalar", "vector")}
        self.mark_ph4 = dict(w0=self.c["w0"], w1=self.c["w1"])
        with self.nc.named_scope("ph5"):
            if mp >= 5:
                self.ph5()
        with self.nc.named_scope("ph6"):
            if mp >= 6:
                self.ph6()
        with self.nc.named_scope("ph2fin"):
            if mp >= 2:
                self.ph2_finish()
        self.fin()

    def pump(self, n):
        if self._ph2g is None:
            return
        for _ in range(n):
            try:
                next(self._ph2g)
            except StopIteration:
                self._ph2g = None
                return

    def ph_init(self):
        d = self.dram
        self.dma("sync", self.ident_sb.ap(), d["ident"], sem="c0")
        self.op("gpsimd", "memset", self.ones_bf.ap(), 1.0, sem="gp")
        self.op("gpsimd", "memset", self.consts.ap()[:, 0:1], float(EPS), sem="gp")
        self.op("gpsimd", "memset", self.consts.ap()[:, 1:2], float(EXP_SHIFT), sem="gp")

    def ph0_gen(self):
        d = self.dram
        jobs = []
        for kk in range(32):
            jobs.append(("wq_c", "wq_s", kk, 1,
                         self.v(self.B, kk * 1024, BF16, 512)))
        for kk0 in range(0, 32, 4):
            jobs.append(("wk_c", "wk_s", kk0, 4,
                         self.v(self.F, kk0 * 256, BF16, 512)))
        for kk0 in range(0, 32, 4):
            jobs.append(("wv_c", "wv_s", kk0, 4,
                         self.v(self.F, 8192 + kk0 * 256, BF16, 512)))
        st = {}
        for idx, (cnm, snm, kk0, stk, dst) in enumerate(jobs):
            par = idx % 2
            cod = self.v(self.A, par * 512, U8, 512)
            scl = self.v(self.A, 1024 + par * 2048, F32, 512)
            aff = self.v(self.A, 5120 + par * 2048, F32, 512)
            src = d[cnm][kk0 * 128:(kk0 + stk) * 128, :]
            srcs = d[snm][kk0 * 128:(kk0 + stk) * 128, :]
            if stk > 1:
                src = src.rearrange("(a p) f -> p a f", p=128)
                srcs = srcs.rearrange("(a p) f -> p a f", p=128)
            if idx >= 2:
                self.wait("sync", "sc", st[("aff", idx - 2)])
                self.wait("sync", "ve", st[("mult", idx - 2)])
            cod3 = cod.rearrange("p (a f) -> p a f", a=stk) if stk > 1 else cod
            scl3 = scl.rearrange("p (a f) -> p a f", a=stk) if stk > 1 else scl
            self.dma("sync", cod3, src, sem=f"p{par}")
            self.dma("sync", scl3, srcs, sem=f"p{par}")
            ldv = self.c[f"p{par}"]
            self.wait("scalar", f"p{par}", ldv)
            if idx >= 2:
                self.wait("scalar", "ve", st[("mult", idx - 2)])
            scv = self.op("scalar", "activation", aff, cod, AF.Copy,
                          bias=-4.0, scale=0.5, sem="sc")
            st[("aff", idx)] = scv
            self.wait("vector", f"p{par}", ldv)
            self.wait("vector", "sc", scv)
            st[("mult", idx)] = self.op(
                "vector", "tensor_tensor", out=dst, in0=aff, in1=scl,
                op=ALU.mult, sem="ve")
            yield

    def pump0(self, n):
        if self._ph0g is None:
            return
        for _ in range(n):
            try:
                next(self._ph0g)
            except StopIteration:
                self._ph0g = None
                return

    def ph0(self):
        while self._ph0g is not None:
            self.pump0(1)
        self.ph0_ve = self.c["ve"]


    def _ln_tile(self, xt, sq, xln, st):
        self.op("vector", "reduce_sum", st[:, 0:1], xt, axis=AX.X, sem="ve")
        self.wait("vector", "ve", self.c["ve"])
        self.op("scalar", "activation", sq, xt, AF.Square,
                accum_out=st[:, 1:2], sem="sc")
        self.wait("vector", "sc", self.c["sc"])
        self.op("vector", "tensor_scalar_mul", st[:, 2:3], st[:, 0:1], 1.0 / DM)
        self.op("vector", "tensor_scalar_mul", st[:, 3:4], st[:, 1:2], 1.0 / DM)
        self.op("vector", "tensor_tensor", out=st[:, 4:5], in0=st[:, 2:3],
                in1=st[:, 2:3], op=ALU.mult, sem="ve")
        self.wait("vector", "ve", self.c["ve"])
        self.op("vector", "tensor_tensor", out=st[:, 5:6], in0=st[:, 3:4],
                in1=st[:, 4:5], op=ALU.subtract, sem="ve")
        self.wait("scalar", "ve", self.c["ve"])
        self.op("scalar", "activation", st[:, 6:7], st[:, 5:6], AF.Sqrt,
                bias=self.consts.ap()[:, 0:1], scale=1.0, sem="sc")
        self.wait("vector", "sc", self.c["sc"])
        self.op("vector", "reciprocal", st[:, 7:8], st[:, 6:7], sem="ve")
        self.wait("vector", "ve", self.c["ve"])
        self.op("vector", "tensor_tensor", out=st[:, 8:9], in0=st[:, 2:3],
                in1=st[:, 7:8], op=ALU.mult, sem="ve")
        self.wait("vector", "ve", self.c["ve"])
        self.op("vector", "tensor_scalar_mul", st[:, 9:10], st[:, 8:9], -1.0,
                sem="ve")
        self.wait("scalar", "ve", self.c["ve"])
        self.op("scalar", "activation", xln, xt, AF.Identity,
                bias=st[:, 9:10], scale=st[:, 7:8], sem="sc")

    def ph1(self):
        d = self.dram
        st = self.stats.ap()
        self.wait("scalar", "gp", 3)
        norm_sc = {}
        ln_te = {}
        tr_ve = {}
        prev_dma_st = 0
        for mt in range(8):
            par = mt % 2
            xt = self.v(self.C, par * 16384, F32, 4096)
            sq = self.v(self.Dr, 0, F32, 4096)
            xln = self.v(self.E, 0, BF16, 4096)
            fmst = self.v(self.E, 8192, BF16, 4096)

            if mt >= 2:
                self.wait("sync", "sc", norm_sc[mt - 2])
            self.dma("sync", xt, d["x_own"][mt * 128:(mt + 1) * 128, :],
                     sem=f"x{par}")
            ldv = self.c[f"x{par}"]
            self.wait("vector", f"x{par}", ldv)
            self.wait("scalar", f"x{par}", ldv)
            if mt >= 1:
                self.wait("scalar", "te", ln_te[mt - 1])
            self._ln_tile(xt, sq, xln, st)
            norm_sc[mt] = self.c["sc"]
            for grp in range(4):
                bank = 4 + (grp % 2)
                pbt = self.psum(bank, 1024, BF16)
                self.wait("tensor", "sc", norm_sc[mt])
                self.wait("tensor", "ve", tr_ve.get(bank, 0))
                for t8 in range(8):
                    kk = grp * 8 + t8
                    self.op("tensor", "transpose",
                            pbt[:, t8 * 128:(t8 + 1) * 128],
                            xln[:, kk * 128:(kk + 1) * 128],
                            self.ident_sb.ap(), sem="te" if t8 == 7 else None)
                tev = self.c["te"]
                self.wait("vector", "te", tev)
                if grp == 0:
                    self.wait("vector", "st1", prev_dma_st)
                tr_ve[bank] = self.op(
                    "vector", "tensor_copy",
                    fmst[:, grp * 1024:(grp + 1) * 1024], pbt, sem="ve")
            ln_te[mt] = self.c["te"]
            seng = "sync" if mt % 2 == 0 else "gpsimd"
            self.wait(seng, "ve", self.c["ve"])
            half = "a" if mt < 4 else "b"
            dst = d[f"xln_own_{half}"].rearrange(
                "p (kk t) -> p kk t", kk=32)[:, :,
                                             (mt % 4) * 128:(mt % 4 + 1) * 128]
            prev_dma_st = self.dma(
                seng, dst,
                fmst.rearrange("p (kk f) -> p kk f", kk=32), sem="st1")
            self.pump(2)
            if mt == 3:
                self.wait("gpsimd", "st1", self.c["st1"])
                self.coll("AllGather", [d["xln_own_a"]], [d["xln_all_a"]])
        self.wait("gpsimd", "st1", self.c["st1"])
        self.coll("AllGather", [d["xln_own_b"]], [d["xln_all_b"]])
        self.dbg("xln_own")
        self.dbg("xln")
        self.ph1_trve = dict(tr_ve)

    SC_T = (0, 1, 3)
    GP_T = (2, 4, 5, 6, 7)

    PH2_N1, PH2_N3, PH2_N4 = 8, 40, 72

    def ph2_width(self, n):
        return 512 if self.PH2_N1 <= n < self.PH2_N3 else 1024

    def ph2_map(self, phase):
        if phase == "ph1":
            base = [(self.A, 16384 + o) for o in
                    (0, 4096, 8192, 12288, 13312, 14336, 15360, 19456, 21504)]
            mp = dict(zip(("p0", "s0", "s1", "m", "cod0", "cod1", "scl",
                           "w0", "w1"), base))
            mp["p0b"] = (self.A, 16384 + 23552)
            return mp
        if phase == "ph3":
            base = [(self.E, 8192), (self.E, 10240), (self.E, 12288),
                    (self.E, 14336), (self.E, 14848), (self.E, 15360),
                    (self.F, 16384), (self.F, 18432), (self.F, 19456)]
            return dict(zip(("p0", "s0", "s1", "m", "cod0", "cod1", "scl",
                             "w0", "w1"), base))
        if phase == "ph4":
            base = [(self.C, o) for o in
                    (0, 4096, 8192, 12288, 13312, 14336, 15360, 19456, 21504)]
            mp = dict(zip(("p0", "s0", "s1", "m", "cod0", "cod1", "scl",
                           "w0", "w1"), base))
            mp["p0b"] = (self.C, 23552)
            return mp
        base = [(self.E, 0), (self.E, 4096), (self.E, 8192),
                (self.E, 12288), (self.E, 13312), (self.E, 14336),
                (self.G, 0), (self.G, 4096), (self.G, 6144)]
        return dict(zip(("p0", "s0", "s1", "m", "cod0", "cod1", "scl",
                         "w0", "w1"), base))

    def _ph2_chunk_jobs(self, k, n0):
        cwk = CHUNK_COLS[k]
        jobs = []
        n = n0
        for cnm, snm, onm in (("g_c", "g_s", f"gw_own{k}"),
                              ("u_c", "u_s", f"uw_own{k}")):
            rt0 = 0
            while rt0 < 32:
                stk = self.ph2_width(n) // cwk
                jobs.append(("gu", cnm, snm, onm, rt0, stk, 256 * k, cwk, 0))
                rt0 += stk
                n += 1
        for rt in range(cwk // 128):
            ci = 0
            while ci < 4096:
                w = self.ph2_width(n)
                jobs.append(("dw", "d_c", "d_s", f"dw_own{k}", 2 * k + rt,
                             1, ci, w, rt))
                ci += w
                n += 1
        return jobs

    def ph2_gen(self):
        d = self.dram
        n = 0
        for k in range(6):
            jobs = self._ph2_chunk_jobs(k, n)
            n += len(jobs)
            for job in jobs:
                yield from self._ph2_job(job)
            self.wait("gpsimd", "w0", self.c["w0"])
            self.wait("gpsimd", "w1", self.c["w1"])
            self.coll("AllGather", [d[f"gw_own{k}"]], [d[f"gw_all{k}"]])
            self.coll("AllGather", [d[f"uw_own{k}"]], [d[f"uw_all{k}"]])
            self.coll("AllGather", [d[f"dw_own{k}"]], [d[f"dw_all{k}"]])
            self.ag_cc[k] = self.c["cc"]

    def _ph2_job(self, job):
        d = self.dram
        kind, cnm, snm, onm, rt0, stk, c0, cw, rtl = job
        idx = self._ph2_idx
        self._ph2_idx += 1
        st = self._ph2_state
        if idx in self._ph2_maps:
            st["map"] = self._ph2_maps[idx]
        mp = st["map"]
        bar = self._ph2_barriers.get(idx)
        if bar:
            for eng, lst in bar.items():
                for semn, val in lst:
                    self.wait(eng, semn, val)
        par = idx % 2
        W1 = stk * cw
        dbp = "p0b" in mp
        p0key = "p0b" if (dbp and par) else "p0"
        p0 = self.v(mp[p0key][0], mp[p0key][1], F32, W1)
        slot = [self.v(mp["s0"][0], mp["s0"][1], F32, W1),
                self.v(mp["s1"][0], mp["s1"][1], F32, W1)]
        m = self.v(mp["m"][0], mp["m"][1], U8, W1)
        cod = self.v(mp[f"cod{par}"][0], mp[f"cod{par}"][1], U8, W1)
        scl = self.v(mp["scl"][0], mp["scl"][1], F32, W1)
        wout = self.v(mp[f"w{par}"][0], mp[f"w{par}"][1], BF16, W1)

        rsl = slice(rt0 * 128, (rt0 + stk) * 128)
        csl = slice(c0, c0 + cw)

        ve0 = self.c["ve"]
        for t in range(1, 8):
            st[("cp", idx, t)] = ve0 + t

        def src3(nm):
            s = d[nm][rsl, csl]
            if stk == 1:
                return s
            return s.rearrange("(a p) f -> p a f", p=128)

        def sb3(view):
            if stk == 1:
                return view
            return view.rearrange("p (a f) -> p a f", a=stk)

        if idx >= 2:
            self.wait("sync", "ve", st[("cp", idx - 2, 7)])
            self.wait("sync", "sc", st[("aff_sc", idx - 2, 7)])
        self.dma("sync", sb3(cod), src3(cnm), sem=f"p{par}")
        st[("cod_ld", idx)] = self.c[f"p{par}"]
        if idx >= 1:
            self.wait("sync", "ve", st[("mult", idx - 1)])
        self.dma("sync", sb3(scl), src3(snm), sem="p2")
        st[("scl_ld", idx)] = self.c["p2"]

        def coef(t):
            dt_ = float(NF4[2 * t + 1] - NF4[2 * t])
            return dt_, float(NF4[2 * t]) - dt_ * (2 * t)

        slot_of = lambda t: slot[(t - 1) % 2]
        def slot_free(eng, t):
            if t >= 3:
                self.wait(eng, "ve", st[("cp", idx, t - 2)])
            elif idx >= 1:
                self.wait(eng, "ve", st[("cp", idx - 1, 8 - t)])

        self.wait("scalar", f"p{par}", st[("cod_ld", idx)])
        pback = 2 if dbp else 1
        if idx >= pback:
            self.wait("scalar", "ve", st[("mult", idx - pback)])
        self.wait("vector", f"p{par}", st[("cod_ld", idx)])
        dt_, bt = coef(0)
        self.op("scalar", "activation", p0, cod, AF.Copy, bias=bt, scale=dt_,
                sem="sc")
        st[("aff_sc", idx, 0)] = self.c["sc"]
        for t in range(1, 8):
            slot_free("scalar", t)
            dt_, bt = coef(t)
            self.op("scalar", "activation", slot_of(t), cod, AF.Copy,
                    bias=bt, scale=dt_, sem="sc")
            st[("aff_sc", idx, t)] = self.c["sc"]
            self.op("vector", "tensor_single_scalar", out=m, in_=cod,
                    scalar=float(2 * t), op=ALU.is_ge)
            self.wait("vector", "sc", st[("aff_sc", idx, t)])
            self.op("vector", "copy_predicated", p0, m, slot_of(t), sem="ve")
            if t == 3:
                yield
        self.wait("vector", "p2", st[("scl_ld", idx)])
        if idx >= 2:
            self.wait("vector", f"w{par}", st[("store", idx - 2)])
        self.op("vector", "tensor_tensor", out=wout, in0=p0, in1=scl,
                op=ALU.mult, sem="ve")
        st[("mult", idx)] = self.c["ve"]

        if kind == "gu":
            dlk = cw // 128
            self.wait("gpsimd", "ve", st[("mult", idx)])
            for dlb in range(dlk):
                dst3 = d[onm].rearrange(
                    "p (dl kk f) -> p dl kk f", dl=dlk, kk=32)[
                    :, dlb:dlb + 1, rt0:rt0 + stk, :]
                srcw = wout.rearrange(
                    "p (a dl f) -> p dl a f", a=stk, dl=dlk, f=128)[
                    :, dlb:dlb + 1, :, :]
                self.dma("gpsimd", dst3, srcw, sem=f"w{par}")
            st[("store", idx)] = self.c[f"w{par}"]
            yield
            return
        else:
            nrt = d[onm].shape[1] // (8 * 512)
            dst3 = d[onm].rearrange("p (hc rt f) -> p hc rt f",
                                    hc=8, rt=nrt)[:, c0 // 512:
                                                  c0 // 512 + cw // 512,
                                                  rtl:rtl + 1, :]
            srcw = wout.rearrange("p (hc u f) -> p hc u f",
                                  hc=cw // 512, u=1)
        self.wait("gpsimd", "ve", st[("mult", idx)])
        self.dma("gpsimd", dst3, srcw, sem=f"w{par}")
        st[("store", idx)] = self.c[f"w{par}"]
        yield

    def ph2_finish(self):
        while self._ph2g is not None:
            self.pump(1)
        self.ph2_done_ve = self.c["ve"]

    def ph3(self):
        d = self.dram
        self.wait("sync", "cc", 1)
        self.wait("gpsimd", "cc", 1)
        self.wait("tensor", "ve", self.ph0_ve)
        self.wait("vector", "gp", self.mark_ph1["gp"])
        self.wait("vector", "sc", self.mark_ph1["sc"])
        self.wait("vector", "w0", self.mark_ph1["w0"])
        self.wait("vector", "w1", self.mark_ph1["w1"])
        qsb = self.v(self.A, 0, BF16, HPC * S)
        ksb = self.v(self.Dr, 0, BF16, S)
        vsb = self.v(self.Dr, 16384, BF16, S)
        cs = self.v(self.G, 0, F32, 2048, parts=64)
        prev_chunk_te = 0
        prev_rope_ve = 0
        bank_free_ve = {4: self.ph1_trve.get(4, 0), 5: self.ph1_trve.get(5, 0)}
        bank_free_sc = {}
        for j in list(range(0, 16, 2)) + list(range(1, 16, 2)):
            self.pump(2)
            if j == 1:
                self.wait("sync", "cc", 2)
            r = j // 2
            xch = self.v(self.C, 0, BF16, 32 * 512)
            self.wait("sync", "te", prev_chunk_te)
            src = d["xln_all_a" if j % 2 == 0 else "xln_all_b"][
                r * 128:(r + 1) * 128, :]
            self.dma("sync", xch, src, sem="c0")
            self.wait("sync", "ve", prev_rope_ve)
            self.dma("sync", cs, d["cs_tab"][:, j * 2048:(j + 1) * 2048],
                     sem="c0")
            ldv = self.c["c0"]
            self.wait("tensor", "c0", ldv)

            for nt in range(4):
                bank = nt % 2
                pq = self.psum(bank)
                self.wait("tensor", "ve", bank_free_ve.get(bank, 0))
                for kk in range(32):
                    self.op("tensor", "matmul", pq,
                            self.v(self.B, kk * 1024 + nt * 256, BF16, 128),
                            xch[:, kk * 512:(kk + 1) * 512],
                            start=(kk == 0), stop=(kk == 31),
                            sem="te" if kk == 31 else None)
                self.wait("vector", "te", self.c["te"])
                qdst = qsb[:, nt * S + j * 512: nt * S + j * 512 + 512]
                bank_free_ve[bank] = self._rope(pq, cs, qdst, scaled=True)
            pk = self.psum(2)
            self.wait("tensor", "ve", bank_free_ve.get(2, 0))
            for kk in range(32):
                self.op("tensor", "matmul", pk,
                        self.v(self.F, kk * 256, BF16, 128),
                        xch[:, kk * 512:(kk + 1) * 512],
                        start=(kk == 0), stop=(kk == 31),
                        sem="te" if kk == 31 else None)
            self.wait("vector", "te", self.c["te"])
            kdst = ksb[:, j * 512:(j + 1) * 512]
            bank_free_ve[2] = self._rope(pk, cs, kdst, scaled=False)
            prev_rope_ve = bank_free_ve[2]
            self.pump(2)
            for pt in range(4):
                bank = 3 + pt % 2
                pv = self.psum(bank, 128)
                self.wait("tensor", "sc", bank_free_sc.get(bank, 0))
                if bank == 4:
                    self.wait("tensor", "ve", bank_free_ve.get(4, 0))
                for kk in range(32):
                    self.op("tensor", "matmul", pv,
                            xch[:, kk * 512 + pt * 128: kk * 512 + (pt + 1) * 128],
                            self.v(self.F, 8192 + kk * 256, BF16, 128),
                            start=(kk == 0), stop=(kk == 31),
                            sem="te" if kk == 31 else None)
                self.wait("scalar", "te", self.c["te"])
                gt = j * 4 + pt
                bank_free_sc[bank] = self.op(
                    "scalar", "copy", vsb[:, gt * 128:(gt + 1) * 128], pv,
                    sem="sc")
                bank_free_ve[4] = bank_free_ve.get(4, 0)
            prev_chunk_te = self.c["te"]
        self.ph3_ve = self.c["ve"]
        self.ph3_sc = self.c["sc"]

    def _rope(self, pq, cs, dst, scaled):
        co = 0 if scaled else 1024
        cosv = cs[0:64, co:co + 512]
        sinv = cs[0:64, co + 512:co + 1024]
        t1 = self.v(self.E, 0, F32, 512, parts=64)
        t2 = self.v(self.E, 2048, F32, 512, parts=64)
        t3 = self.v(self.E, 4096, F32, 512, parts=64)
        t4 = self.v(self.E, 6144, F32, 512, parts=64)
        p1 = pq[0:64, :]
        p2 = pq[64:128, :]
        self.op("vector", "tensor_tensor", out=t1, in0=p1, in1=cosv, op=ALU.mult)
        self.op("vector", "tensor_tensor", out=t2, in0=p2, in1=sinv, op=ALU.mult)
        self.op("vector", "tensor_tensor", out=t3, in0=p2, in1=cosv, op=ALU.mult)
        self.op("vector", "tensor_tensor", out=t4, in0=p1, in1=sinv, op=ALU.mult)
        self.op("vector", "tensor_tensor", out=dst[0:64, :], in0=t1, in1=t2,
                op=ALU.subtract)
        return self.op("vector", "tensor_tensor", out=dst[64:128, :], in0=t3,
                       in1=t4, op=ALU.add, sem="ve")

    def ph4(self):
        d = self.dram
        self.ph4_tebar = self.c["te"]
        qsb = self.v(self.A, 0, BF16, HPC * S)
        ksb = self.v(self.Dr, 0, BF16, S)
        vsb = self.v(self.Dr, 16384, BF16, S)
        msk = self.v(self.G, 0, F32, 2048)
        self.wait("sync", "ve", self.ph3_ve)
        self.dma("sync", msk, d["masks"], sem="c0")
        ldm = self.c["c0"]
        self.wait("tensor", "c0", ldm)
        self.wait("tensor", "ve", self.ph3_ve)
        self.wait("tensor", "sc", self.ph3_sc)
        self.wait("scalar", "ve", self.ph3_ve)

        exp_done = {}
        pt_free = {}
        norm_ve = 0
        store_st = 0
        self.pre_deq_ve = {}
        pre_mult = {}
        woc = self.v(self.B, 0, BF16, 32 * 512)

        def pre_deq(kk):
            slot = kk % 3
            wsem = ("p3", "p4", "p5")[slot]
            cod = self.v(self.F, slot * 512, U8, 512)
            scl = self.v(self.F, 1536 + slot * 2048, F32, 512)
            aff = self.v(self.F, 7680 + slot * 2048, F32, 512)
            if kk < 3:
                self.wait("sync", "te", self.ph4_tebar)
            else:
                self.wait("sync", "ve", pre_mult[kk - 3])
            self.dma("sync", cod, d["wo_c"][kk * 128:(kk + 1) * 128, 0:512],
                     sem=wsem)
            self.dma("sync", scl, d["wo_s"][kk * 128:(kk + 1) * 128, 0:512],
                     sem=wsem)
            ldv = self.c[wsem]
            self.wait("scalar", wsem, ldv)
            if kk >= 3:
                self.wait("scalar", "ve", pre_mult[kk - 3])
            scv = self.op("scalar", "activation", aff, cod, AF.Copy,
                          bias=-4.0, scale=0.5, sem="sc")
            self.wait("vector", wsem, ldv)
            self.wait("vector", "sc", scv)
            if kk == 0:
                self.wait("vector", "te", self.ph4_tebar)
            pre_mult[kk] = self.op(
                "vector", "tensor_tensor", out=woc[:, kk * 512:(kk + 1) * 512],
                in0=aff, in1=scl, op=ALU.mult, sem="ve")
            self.pre_deq_ve[kk] = pre_mult[kk]

        for pair in range(2):
            for j in range(16):
                self.pump(1)
                pre_deq(pair * 16 + j)
                nkv = 4 * (j + 1)
                sc_of = {}
                te_s_of = {}
                for idx in range(nkv + 1):
                    if idx < nkv:
                        par = idx % 2
                        for hl in range(2):
                            bank = 2 * hl + par
                            pss = self.psum(bank)
                            self.wait("tensor", "sc",
                                      exp_done.get((hl, par), 0))
                            if idx == 0 and hl == 0:
                                self.wait("tensor", "ve", norm_ve)
                            h = pair * 2 + hl
                            te_s_of[(idx, hl)] = self.op(
                                "tensor", "matmul", pss,
                                ksb[:, idx * 128:(idx + 1) * 128],
                                qsb[:, h * S + j * 512: h * S + j * 512 + 512],
                                start=True, stop=True, sem="te")
                    if idx >= 1:
                        t = idx - 1
                        par = t % 2
                        diag = (t // 4 == j)
                        for hl in range(2):
                            bank = 2 * hl + par
                            pss = self.psum(bank)
                            ptv = self.v(self.E, (2 * hl + par) * 1024, BF16, 512)
                            tev = te_s_of[(t, hl)]
                            if diag:
                                t_in = t % 4
                                self.wait("vector", "te", tev)
                                vev = self.op(
                                    "vector", "tensor_tensor", out=pss, in0=pss,
                                    in1=msk[:, t_in * 512:(t_in + 1) * 512],
                                    op=ALU.add, sem="ve")
                                self.wait("scalar", "ve", vev)
                            else:
                                self.wait("scalar", "te", tev)
                            self.wait("scalar", "te", pt_free.get((hl, par), 0))
                            sc_of[(t, hl)] = self.op(
                                "scalar", "activation", ptv, pss, AF.Exp,
                                bias=self.consts.ap()[:, 1:2], scale=1.0,
                                sem="sc")
                            exp_done[(hl, par)] = sc_of[(t, hl)]
                            self.wait("tensor", "sc", sc_of[(t, hl)])
                            self.op("tensor", "matmul", self.psum(4 + hl),
                                    vsb[:, t * 128:(t + 1) * 128], ptv,
                                    start=(t == 0), stop=(t == nkv - 1),
                                    skip_group_check=True)
                            pt_free[(hl, par)] = self.op(
                                "tensor", "matmul", self.psum(6 + hl),
                                self.ones_bf.ap(), ptv,
                                start=(t == 0), stop=(t == nkv - 1),
                                skip_group_check=True, sem="te")
                self.wait("vector", "te", self.c["te"])
                self.wait("vector", "as", store_st)
                for hl in range(2):
                    rec = self.v(self.E, 4096, F32, 512)
                    astg = self.v(self.E, 6144 + hl * 1024, BF16, 512)
                    self.op("vector", "reciprocal", rec, self.psum(6 + hl),
                            sem="ve")
                    self.wait("vector", "ve", self.c["ve"])
                    self.op("vector", "tensor_tensor", out=astg,
                            in0=self.psum(4 + hl), in1=rec, op=ALU.mult,
                            sem="ve")
                norm_ve = self.c["ve"]
                self.wait("sync", "ve", norm_ve)
                astg2 = self.v(self.E, 6144, BF16, 1024)
                r, off = j // 2, (j % 2) * 512
                dst = d[f"attn_a2a_p{pair}"][r * 128:(r + 1) * 128, :]\
                    .rearrange("p (kl t) -> p kl t", kl=2)[:, :, off:off + 512]
                store_st = self.dma(
                    "sync", dst,
                    astg2.rearrange("p (h f) -> p h f", h=2), sem="as")
                self.pump(1)
            self.wait("gpsimd", "as", self.c["as"])
            self.coll("AllToAll", [d[f"attn_a2a_p{pair}"]],
                      [d[f"attn_loc_p{pair}"]])
            if pair == 0:
                self.a2a_cc_a = self.c["cc"]
        self.a2a_cc = self.c["cc"]
        self.dbg("a2a_in")
        self.dbg("attn")
        self.ph4_norm_ve = norm_ve

    def ph5(self):
        d = self.dram
        self.wait("sync", "cc", self.a2a_cc_a)
        self.wait("sync", "ve", self.c["ve"])
        self.wait("sync", "w0", self.c["w0"])
        self.wait("sync", "w1", self.c["w1"])
        for cslab in range(8):
            self.dma("sync", self.v(self.A, cslab * 8192, BF16, 2048),
                     d["attn_loc_p0"][cslab * 128:(cslab + 1) * 128, :],
                     sem="c0")
        slab_ld_a = self.c["c0"]
        self.wait("sync", "cc", self.a2a_cc)
        for cslab in range(8):
            self.dma("sync", self.v(self.A, cslab * 8192 + 4096, BF16, 2048),
                     d["attn_loc_p1"][cslab * 128:(cslab + 1) * 128, :],
                     sem="c0")
        slab_ld = self.c["c0"]

        def slab_ap(kk, mt):
            return self.v(self.A, (kk // 4) * 8192 + (kk % 4) * 2048
                          + mt * 256, BF16, 128)

        woc = self.v(self.B, 0, BF16, 32 * 512)
        deq_ve = {}
        woc_te = {}
        oproj_te = 0
        ostore_ad = 0
        prev_evac_ve = self.ph4_norm_ve
        WOSEM = ("p3", "p4", "p5", "g0")
        KK_ORDER = ([4 * cc + kl for cc in range(8) for kl in (0, 1)] +
                    [4 * cc + kl for cc in range(8) for kl in (2, 3)])
        for ncb in range(8):
            for pos, kk in enumerate(KK_ORDER):
                if ncb == 0:
                    if pos == 0:
                        self.wait("tensor", "c0", slab_ld_a)
                        self.wait("tensor", "ve", prev_evac_ve)
                    if pos == 16:
                        self.wait("tensor", "c0", slab_ld)
                    self.wait("tensor", "ve", self.pre_deq_ve[kk])
                else:
                    slot = pos % 4
                    wsem = WOSEM[slot]
                    cod = self.v(self.C, slot * 512, U8, 512)
                    scl = self.v(self.C, 2048 + slot * 2048, F32, 512)
                    aff = self.v(self.C, 10240 + slot * 2048, F32, 512)
                    self.wait("sync", "ve", deq_ve.get(slot, 0))
                    csl = slice(ncb * 512, (ncb + 1) * 512)
                    rsl = slice(kk * 128, (kk + 1) * 128)
                    self.dma("sync", cod, d["wo_c"][rsl, csl], sem=wsem)
                    self.dma("sync", scl, d["wo_s"][rsl, csl], sem=wsem)
                    ldv = self.c[wsem]
                    self.wait("scalar", wsem, ldv)
                    self.wait("scalar", "ve", deq_ve.get(slot, 0))
                    scv = self.op("scalar", "activation", aff, cod, AF.Copy,
                                  bias=-4.0, scale=0.5, sem="sc")
                    self.wait("vector", "sc", scv)
                    self.wait("vector", "te", woc_te.get(kk, 0))
                    deq_ve[slot] = self.op(
                        "vector", "tensor_tensor",
                        out=woc[:, kk * 512:(kk + 1) * 512],
                        in0=aff, in1=scl, op=ALU.mult, sem="ve")
                    if pos == 0:
                        self.wait("tensor", "ve", prev_evac_ve)
                    self.wait("tensor", "ve", deq_ve[slot])
                for mt in range(8):
                    self.op("tensor", "matmul", self.psum(mt),
                            slab_ap(kk, mt),
                            woc[:, kk * 512:(kk + 1) * 512],
                            start=(pos == 0), stop=(pos == 31),
                            skip_group_check=True,
                            sem="te" if mt == 7 else None)
                woc_te[kk] = self.c["te"]
            oproj_te = self.c["te"]
            xcol = self.v(self.Dr, 0, F32, 8 * 512)
            src = d["x_own"][:, ncb * 512:(ncb + 1) * 512]
            self.wait("sync", "ve", prev_evac_ve)
            self.dma("sync", xcol.rearrange("p (mt f) -> p mt f", mt=8),
                     src.rearrange("(mt p) f -> p mt f", p=128), sem="c0")
            xldv = self.c["c0"]
            ostg = self.v(self.Dr, 16384, F32, 8 * 512)
            self.wait("vector", "te", oproj_te)
            self.wait("vector", "c0", xldv)
            self.wait("vector", "ad", ostore_ad)
            for mt in range(8):
                self.op("vector", "tensor_tensor",
                        out=ostg[:, mt * 512:(mt + 1) * 512],
                        in0=self.psum(mt), in1=xcol[:, mt * 512:(mt + 1) * 512],
                        op=ALU.add, sem="ve" if mt == 7 else None)
            prev_evac_ve = self.c["ve"]
            self.wait("gpsimd", "ve", prev_evac_ve)
            dst = d["out"][:, ncb * 512:(ncb + 1) * 512]
            ostore_ad = self.dma(
                "gpsimd", dst.rearrange("(mt p) f -> p mt f", p=128),
                ostg.rearrange("p (mt f) -> p mt f", mt=8), sem="ad")
        self.ph5_evac_ve = prev_evac_ve

        st = self.stats.ap()
        self.wait("sync", "ad", self.c["ad"])
        ln2_sc = {}
        ln_te = {}
        tr_ve = {4: prev_evac_ve, 5: prev_evac_ve}
        for mt in range(8):
            self.pump(2)
            par = mt % 2
            xt = self.v(self.C, par * 16384, F32, 4096)
            sq = self.v(self.B, 0, F32, 4096)
            xln = self.v(self.F, par * 8192, BF16, 4096)
            if mt >= 2:
                self.wait("sync", "sc", ln2_sc[mt - 2])
            self.dma("sync", xt, d["out"][mt * 128:(mt + 1) * 128, :],
                     sem=f"x{par}")
            ldv = self.c[f"x{par}"]
            self.wait("vector", f"x{par}", ldv)
            self.wait("scalar", f"x{par}", ldv)
            if mt >= 2:
                self.wait("scalar", "te", ln_te[mt - 2])
            self._ln_tile(xt, sq, xln, st)
            ln2_sc[mt] = self.c["sc"]
            for grp in range(4):
                bank = 4 + (grp % 2)
                pbt = self.psum(bank, 1024, BF16)
                self.wait("tensor", "sc", ln2_sc[mt])
                self.wait("tensor", "ve", tr_ve.get(bank, 0))
                for t8 in range(8):
                    kk = grp * 8 + t8
                    self.op("tensor", "transpose",
                            pbt[:, t8 * 128:(t8 + 1) * 128],
                            xln[:, kk * 128:(kk + 1) * 128],
                            self.ident_sb.ap(), sem="te" if t8 == 7 else None)
                tev = self.c["te"]
                self.wait("vector", "te", tev)
                for t8 in range(8):
                    kk = grp * 8 + t8
                    self.op("vector", "tensor_copy",
                            self.v(self.A, kk * 2048 + mt * 256, BF16, 128),
                            pbt[:, t8 * 128:(t8 + 1) * 128],
                            sem="ve" if t8 == 7 else None)
                tr_ve[bank] = self.c["ve"]
            ln_te[mt] = self.c["te"]
        self.ph5_ln2_ve = self.c["ve"]
        self.mark_ph5 = dict(sc=self.c["sc"], ve=self.c["ve"],
                             te=self.c["te"], ad=self.c["ad"])
        self.dbg("x2")

    def ph6(self):
        d = self.dram
        self.wait("scalar", "w0", self.mark_ph4["w0"])
        self.wait("scalar", "w1", self.mark_ph4["w1"])
        xln2 = lambda kk, mc: self.v(self.A, kk * 2048 + mc * 1024, BF16, 512)
        inter = self.v(self.B, 0, BF16, 16 * 1024)
        gu_te = {}
        inter_ve = {0: self.ph5_ln2_ve, 1: self.ph5_ln2_ve}
        dsl_te = {}
        evac_ve = 0
        evac_half = {}
        half_te = {}
        add_ad = 0
        down_te_last = 0
        for sbi in range(6):
            tiles = SB_TILES[sbi]
            nsb = len(tiles)
            while sbi not in self.ag_cc and self._ph2g is not None:
                self.pump(1)
            ccv = self.ag_cc[sbi]
            for il, (r, dl) in enumerate(tiles):
                par = il % 2
                gsl = self.v(self.C, par * 8192, BF16, 32 * 128)
                usl = self.v(self.C, 16384 + par * 8192, BF16, 32 * 128)
                self.wait("sync", "cc", ccv)
                self.wait("sync", "te", gu_te.get(par, 0))
                if sbi == 0 and il <= 1:
                    self.wait("sync", "sc", self.mark_ph5["sc"])
                    self.wait("sync", "ve", self.mark_ph5["ve"])
                gsrc = d[f"gw_all{sbi}"][r * 128:(r + 1) * 128,
                                         dl * DM:(dl + 1) * DM]
                usrc = d[f"uw_all{sbi}"][r * 128:(r + 1) * 128,
                                         dl * DM:(dl + 1) * DM]
                self.dma("sync", gsl, gsrc, sem=f"g{par}")
                self.dma("sync", usl, usrc, sem=f"g{par}")
                ldv = self.c[f"g{par}"]
                bb = 4 * par
                self.pump(4)
                self.wait("tensor", f"g{par}", ldv)
                self.wait("tensor", "ve", inter_ve.get(par, 0))
                if il <= 1:
                    self.wait("tensor", "ve", evac_ve)
                    if sbi == 0:
                        self.wait("tensor", "ve", self.ph5_ln2_ve)
                for kk in range(32):
                    for mc in range(2):
                        self.op("tensor", "matmul", self.psum(bb + mc),
                                gsl[:, kk * 128:(kk + 1) * 128],
                                xln2(kk, mc), start=(kk == 0), stop=(kk == 31),
                                skip_group_check=True)
                for kk in range(32):
                    for mc in range(2):
                        self.op("tensor", "matmul", self.psum(bb + 2 + mc),
                                usl[:, kk * 128:(kk + 1) * 128],
                                xln2(kk, mc), start=(kk == 0), stop=(kk == 31),
                                skip_group_check=True,
                                sem="te" if kk == 31 and mc == 1 else None)
                gu_te[par] = self.c["te"]
                for mc in range(2):
                    sstg = self.v(self.F, 16384 + (2 * par + mc) * 1024, BF16,
                                  512)
                    self.wait("scalar", "te", gu_te[par])
                    scv = self.op("scalar", "activation", sstg,
                                  self.psum(bb + mc),
                                  AF.Silu, bias=0.0, scale=1.0, sem="sc")
                    self.wait("vector", "sc", scv)
                    if sbi > 0 and il == 0 and mc == 0:
                        self.wait("vector", "te", down_te_last)
                    self.op("vector", "tensor_tensor",
                            out=inter[:, il * 1024 + mc * 512:
                                      il * 1024 + (mc + 1) * 512],
                            in0=self.psum(bb + 2 + mc), in1=sstg, op=ALU.mult,
                            sem="ve")
                inter_ve[par] = self.c["ve"]
            allint_ve = self.c["ve"]
            nrt = CHUNK_COLS[sbi] // 128
            for hc in range(8):
                par = hc % 2
                dsl = self.v(self.Dr, par * 16384, BF16, nsb * 512)
                self.wait("sync", "te", dsl_te.get(par, 0))
                if sbi == 0 and hc <= 1:
                    self.wait("sync", "te", self.mark_ph5["te"])
                    self.wait("sync", "ve", self.mark_ph5["ve"])
                    self.wait("sync", "ad", self.mark_ph5["ad"])
                dsem = f"x{par}"
                dsrc = d[f"dw_all{sbi}"].rearrange(
                    "(r p) (hc c) -> p r hc c", p=128, hc=8)[:, :,
                                                             hc:hc + 1, :]
                self.dma("sync",
                         self.v(self.Dr, par * 16384, BF16, 8 * nrt * 512)
                         .rearrange("p (r u c) -> p r u c", r=8, u=1),
                         dsrc, sem=dsem)
                ldv = self.c[dsem]
                self.pump(4)
                self.wait("tensor", dsem, ldv)
                self.wait("tensor", "ve", allint_ve)
                dstg = self.v(self.F, 0, F32, 8 * 512)
                for half in range(2):
                    mts = range(4 * half, 4 * half + 4)
                    self.wait("tensor", "ve", evac_half.get(half, 0))
                    for il, (r, dl) in enumerate(tiles):
                        iti = r * nrt + dl
                        for mt in mts:
                            self.op("tensor", "matmul", self.psum(mt),
                                    inter[:, il * 1024 + mt * 128:
                                          il * 1024 + mt * 128 + 128],
                                    dsl[:, iti * 512:(iti + 1) * 512],
                                    start=(il == 0), stop=(il == nsb - 1),
                                    skip_group_check=True,
                                    sem="te" if il == nsb - 1 and mt == mts[-1]
                                    else None)
                    half_te[half] = self.c["te"]
                    self.wait("vector", "te", half_te[half])
                    if half == 0:
                        self.wait("vector", "ad", add_ad)
                    for mt in mts:
                        self.op("vector", "tensor_copy",
                                dstg[:, mt * 512:(mt + 1) * 512], self.psum(mt),
                                sem="ve" if mt == mts[-1] else None)
                    evac_half[half] = self.c["ve"]
                dsl_te[par] = self.c["te"]
                down_te_last = self.c["te"]
                evac_ve = self.c["ve"]
                self.wait("gpsimd", "ve", evac_ve)
                dstd = d["out"][:, hc * 512:(hc + 1) * 512]
                add_ad = self.dma(
                    "gpsimd", dstd.rearrange("(mt p) f -> p mt f", p=128),
                    dstg.rearrange("p (mt f) -> p mt f", mt=8), sem="ad",
                    accum=ALU.add)

    def fin(self):
        if "wq" in self.debug:
            self.wait("sync", "ve", self.c["ve"])
            self.dma("sync", self.dram["dbg_wq"],
                     self.v(self.B, 0, BF16, 32 * 512), sem="st1")
        if "qkv" in self.debug and self.max_phase in (3, 4):
            self.wait("sync", "ve", self.c["ve"])
            self.wait("sync", "sc", self.c["sc"])
            self.wait("sync", "te", self.c["te"])
            self.dma("sync", self.dram["dbg_q"],
                     self.v(self.A, 0, BF16, HPC * S), sem="st1")
            self.dma("sync", self.dram["dbg_k"],
                     self.v(self.Dr, 0, BF16, S), sem="st1")
            self.dma("sync", self.dram["dbg_v"],
                     self.v(self.Dr, 16384, BF16, S), sem="st1")
        for tag in self.dbg_jobs:
            for sm in ("st1", "as", "w0", "w1", "ad", "cc"):
                self.wait("sync", sm, self.c[sm])
            self.dma("sync", self.dram["dbg_" + tag],
                     self.dram[DBG_TAGS[tag]], sem="st1")
        self.wait("gpsimd", "ad", self.c["ad"])
        self.wait("sync", "st1", self.c["st1"])
        self.wait("sync", "as", self.c["as"])
        self.wait("sync", "ad", self.c["ad"])



def build_nc(max_phase=6, debug=()):
    nc = bass.Bass("TRN2", target_bir_lowering=False)
    b = Builder(nc, max_phase=max_phase, debug=debug)
    b.build()
    return nc, b


def kernel(**inputs):
    in_maps = host_prep(inputs)
    nc, b = build_nc()
    res = run_bass_kernel_spmd(nc, in_maps, core_ids=list(range(W)))
    out = np.concatenate([np.asarray(r["out"]) for r in res.results], axis=0)
    return out.reshape(1, S, DM).astype(np.float32)


# version 17
# speedup vs baseline: 1.0107x; 1.0107x over previous
import math
import numpy as np
import ml_dtypes
import concourse.bass as bass
import concourse.mybir as mybir
from concourse.bass_utils import run_bass_kernel_spmd
from concourse.aot_env import allow_ici_all_to_all

F32 = mybir.dt.float32
BF16 = mybir.dt.bfloat16
U8 = mybir.dt.uint8
AF = mybir.ActivationFunctionType
ALU = mybir.AluOpType
AX = mybir.AxisListType

S = 8192
DM = 4096
NH, KVH, D = 32, 8, 128
I_FULL = 11008
I_PAD = 11264
ITC = I_PAD // 8
W = 8
TOK = S // W
HPC = NH // W
EXP_SHIFT = -20.0
EPS = 1e-5
N_ITILES = I_FULL // 128
CHUNK_COLS = [256, 256, 256, 256, 256, 128]
SB_TILES = []
for _k in range(6):
    _tiles = []
    for _dl in range(CHUNK_COLS[_k] // 128):
        _l = 2 * _k + _dl
        for _r in range(8):
            if _r * 11 + _l < 86:
                _tiles.append((_r, _dl))
    SB_TILES.append(_tiles)

NF4 = np.array([-1.0, -0.6961928009986877, -0.5250730514526367, -0.39491748809814453,
                -0.28444138169288635, -0.18477343022823334, -0.09105003625154495, 0.0,
                0.07958029955625534, 0.16093020141124725, 0.24611230194568634,
                0.33791524171829224, 0.44070982933044434, 0.5626170039176941,
                0.7229568362236023, 1.0], dtype=np.float32)

DT_BYTES = {F32: 4, BF16: 2, U8: 1}

DBG_TAGS = {"x2": "out", "xln_own": "xln_own_a", "xln": "xln_all_a", "attn": "attn_loc_p0", "a2a_in": "attn_a2a_p0"}



def host_prep(inputs):
    x = np.ascontiguousarray(np.asarray(inputs["L0_attn_x"]).reshape(S, DM),
                             dtype=np.float32)

    def u8(a):
        return np.ascontiguousarray(np.asarray(a).astype(np.uint8))

    def exp_fp4(s):
        return np.repeat(np.ascontiguousarray(np.asarray(s), dtype=np.float32),
                         128, axis=0)

    qc, kc, vc = (u8(inputs["L0_attn_q_codes"]), u8(inputs["L0_attn_k_codes"]),
                  u8(inputs["L0_attn_v_codes"]))
    oc = u8(inputs["L0_attn_o_codes"])
    qs, ks, vs = (exp_fp4(inputs["L0_attn_q_scales"]),
                  exp_fp4(inputs["L0_attn_k_scales"]),
                  exp_fp4(inputs["L0_attn_v_scales"]))
    os_ = exp_fp4(inputs["L0_attn_o_scales"])

    def padi(a, axis):
        pad = [(0, 0), (0, 0)]
        pad[axis] = (0, I_PAD - I_FULL)
        return np.pad(a, pad)

    gT = padi(u8(np.asarray(inputs["L0_ffn_gate_codes"]).T), 1)
    uT = padi(u8(np.asarray(inputs["L0_ffn_up_codes"]).T), 1)
    dT = padi(u8(np.asarray(inputs["L0_ffn_down_codes"]).T), 0)
    gsT = padi(np.repeat(np.ascontiguousarray(
        np.asarray(inputs["L0_ffn_gate_scales"]).T, dtype=np.float32), 64, axis=0), 1)
    usT = padi(np.repeat(np.ascontiguousarray(
        np.asarray(inputs["L0_ffn_up_scales"]).T, dtype=np.float32), 64, axis=0), 1)
    dsT = padi(np.repeat(np.ascontiguousarray(
        np.asarray(inputs["L0_ffn_down_scales"]).T, dtype=np.float32), 64, axis=0), 0)

    half = D // 2
    inv_freq = (1.0 / (10000.0 ** (np.arange(half, dtype=np.float32) / half))).astype(np.float32)
    pos = np.arange(S, dtype=np.float32)
    ang = (inv_freq[:, None] * pos[None, :]).astype(np.float32)
    cosv = np.cos(ang).astype(np.float32)
    sinv = np.sin(ang).astype(np.float32)
    rs = np.float32(1.0 / math.sqrt(D))
    cs_packed = np.zeros((half, 16 * 2048), dtype=np.float32)
    for j in range(16):
        sl = slice(j * 512, (j + 1) * 512)
        cs_packed[:, j * 2048:j * 2048 + 512] = cosv[:, sl] * rs
        cs_packed[:, j * 2048 + 512:j * 2048 + 1024] = sinv[:, sl] * rs
        cs_packed[:, j * 2048 + 1024:j * 2048 + 1536] = cosv[:, sl]
        cs_packed[:, j * 2048 + 1536:j * 2048 + 2048] = sinv[:, sl]

    masks = np.zeros((128, 4 * 512), dtype=np.float32)
    pp = np.arange(128)[:, None]
    ff = np.arange(512)[None, :]
    for t in range(4):
        masks[:, t * 512:(t + 1) * 512] = np.where(128 * t + pp <= ff, 0.0, -1e30)

    ident = np.eye(128, dtype=np.float32).astype(ml_dtypes.bfloat16)

    in_maps = []
    for c in range(W):
        in_maps.append({
            "x_own": np.ascontiguousarray(x[c * TOK:(c + 1) * TOK]),
            "wq_c": np.ascontiguousarray(qc[:, c * 512:(c + 1) * 512]),
            "wq_s": np.ascontiguousarray(qs[:, c * 512:(c + 1) * 512]),
            "wk_c": np.ascontiguousarray(kc[:, c * 128:(c + 1) * 128]),
            "wk_s": np.ascontiguousarray(ks[:, c * 128:(c + 1) * 128]),
            "wv_c": np.ascontiguousarray(vc[:, c * 128:(c + 1) * 128]),
            "wv_s": np.ascontiguousarray(vs[:, c * 128:(c + 1) * 128]),
            "wo_c": oc, "wo_s": os_,
            "g_c": np.ascontiguousarray(gT[:, c * ITC:(c + 1) * ITC]),
            "g_s": np.ascontiguousarray(gsT[:, c * ITC:(c + 1) * ITC]),
            "u_c": np.ascontiguousarray(uT[:, c * ITC:(c + 1) * ITC]),
            "u_s": np.ascontiguousarray(usT[:, c * ITC:(c + 1) * ITC]),
            "d_c": np.ascontiguousarray(dT[c * ITC:(c + 1) * ITC, :]),
            "d_s": np.ascontiguousarray(dsT[c * ITC:(c + 1) * ITC, :]),
            "cs_tab": cs_packed, "masks": masks, "ident": ident,
        })
    return in_maps



class Builder:
    SEMS = ("c0", "x0", "x1", "st1", "p0", "p1", "p2", "p3", "p4", "p5",
            "w0", "w1", "g0", "g1", "as", "ad", "gp", "te", "ve", "sc", "cc")

    def __init__(self, nc, max_phase=6, debug=()):
        self.nc = nc
        self.max_phase = max_phase
        self.debug = set(debug)
        self.cur = None
        self.eng = None
        self.sems = {}
        self.dram = {}
        self.ps = {}

    def _reset(self):
        self.c = {s: 0 for s in self.SEMS}
        self.dbg_jobs = []

    def _inc(self, inst, name, n):
        if inst is not None:
            inst.then_inc(self.sems[name], n)
        self.c[name] += n
        return self.c[name]

    def wait(self, engname, name, val=None):
        v = self.c[name] if val is None else val
        if self.cur == engname and v > 0:
            self.eng.wait_ge(self.sems[name], int(v))
        return v

    def dma(self, engname, out, in_, sem, accum=None):
        inst = None
        if self.cur == engname:
            kw = {}
            if accum is not None:
                kw["accum_op"] = accum
            inst = self.eng.dma_start(out=out, in_=in_, **kw)
        return self._inc(inst, sem, 16)

    def op(self, engname, meth, *a, sem=None, n=1, **kw):
        inst = None
        if self.cur == engname:
            inst = getattr(self.eng, meth)(*a, **kw)
        if sem is not None:
            return self._inc(inst, sem, n)
        return None

    def coll(self, kind, ins, outs):
        inst = None
        if self.cur == "gpsimd":
            with allow_ici_all_to_all():
                inst = self.eng.collective_compute(
                    kind, ALU.bypass, replica_groups=[list(range(W))],
                    ins=ins, outs=outs)
        return self._inc(inst, "cc", 1)

    def alloc(self):
        nc = self.nc
        dd = self.dram
        io = [("x_own", [TOK, DM], F32, "ExternalInput"),
              ("wq_c", [DM, 512], U8, "ExternalInput"),
              ("wq_s", [DM, 512], F32, "ExternalInput"),
              ("wk_c", [DM, 128], U8, "ExternalInput"),
              ("wk_s", [DM, 128], F32, "ExternalInput"),
              ("wv_c", [DM, 128], U8, "ExternalInput"),
              ("wv_s", [DM, 128], F32, "ExternalInput"),
              ("wo_c", [DM, DM], U8, "ExternalInput"),
              ("wo_s", [DM, DM], F32, "ExternalInput"),
              ("g_c", [DM, ITC], U8, "ExternalInput"),
              ("g_s", [DM, ITC], F32, "ExternalInput"),
              ("u_c", [DM, ITC], U8, "ExternalInput"),
              ("u_s", [DM, ITC], F32, "ExternalInput"),
              ("d_c", [ITC, DM], U8, "ExternalInput"),
              ("d_s", [ITC, DM], F32, "ExternalInput"),
              ("cs_tab", [64, 16 * 2048], F32, "ExternalInput"),
              ("masks", [128, 4 * 512], F32, "ExternalInput"),
              ("ident", [128, 128], BF16, "ExternalInput"),
              ("out", [TOK, DM], F32, "ExternalOutput"),
              ("xln_own_a", [128, 32 * (TOK // 2)], BF16, "Internal"),
              ("xln_own_b", [128, 32 * (TOK // 2)], BF16, "Internal"),
              ("xln_all_a", [W * 128, 32 * (TOK // 2)], BF16, "Internal"),
              ("xln_all_b", [W * 128, 32 * (TOK // 2)], BF16, "Internal"),
              ("attn_a2a_p0", [W * 128, 2 * TOK], BF16, "Internal"),
              ("attn_a2a_p1", [W * 128, 2 * TOK], BF16, "Internal"),
              ("attn_loc_p0", [W * 128, 2 * TOK], BF16, "Internal"),
              ("attn_loc_p1", [W * 128, 2 * TOK], BF16, "Internal")]
        for k in range(6):
            cwk = CHUNK_COLS[k]
            dlk = cwk // 128
            io.append((f"gw_own{k}", [128, dlk * DM], BF16, "Internal"))
            io.append((f"uw_own{k}", [128, dlk * DM], BF16, "Internal"))
            io.append((f"gw_all{k}", [W * 128, dlk * DM], BF16, "Internal"))
            io.append((f"uw_all{k}", [W * 128, dlk * DM], BF16, "Internal"))
            io.append((f"dw_own{k}", [128, 32 * cwk], BF16, "Internal"))
            io.append((f"dw_all{k}", [W * 128, 32 * cwk], BF16, "Internal"))
        for nm, shp, dt_, kind in io:
            dd[nm] = nc.dram_tensor(nm, shp, dt_, kind=kind).ap()
        if "wq" in self.debug:
            dd["dbg_wq"] = nc.dram_tensor("dbg_wq", [128, 32 * 512], BF16,
                                          kind="ExternalOutput").ap()
        if "qkv" in self.debug:
            dd["dbg_q"] = nc.dram_tensor("dbg_q", [128, HPC * S], BF16,
                                         kind="ExternalOutput").ap()
            dd["dbg_k"] = nc.dram_tensor("dbg_k", [128, S], BF16,
                                         kind="ExternalOutput").ap()
            dd["dbg_v"] = nc.dram_tensor("dbg_v", [128, S], BF16,
                                         kind="ExternalOutput").ap()
        for tag in self.debug:
            if tag in ("qkv", "wq"):
                continue
            src = dd[DBG_TAGS[tag]]
            dd["dbg_" + tag] = nc.dram_tensor(
                "dbg_" + tag, list(src.shape), src.dtype,
                kind="ExternalOutput").ap()

        self.A = nc.alloc_sbuf_tensor("arA", [128, 65536], U8)
        self.B = nc.alloc_sbuf_tensor("arB", [128, 32768], U8)
        self.C = nc.alloc_sbuf_tensor("arC", [128, 32768], U8)
        self.Dr = nc.alloc_sbuf_tensor("arD", [128, 32768], U8)
        self.E = nc.alloc_sbuf_tensor("arE", [128, 16384], U8)
        self.F = nc.alloc_sbuf_tensor("arF", [128, 20480], U8)
        self.G = nc.alloc_sbuf_tensor("arG", [128, 8192], U8)
        self.ones_bf = nc.alloc_sbuf_tensor("ones_bf", [128, 128], BF16)
        self.consts = nc.alloc_sbuf_tensor("consts", [128, 2], F32)
        self.ident_sb = nc.alloc_sbuf_tensor("ident_sb", [128, 128], BF16)
        self.stats = nc.alloc_sbuf_tensor("stats", [128, 64], F32)
        for i in range(8):
            self.ps[i] = nc.alloc_psum_tensor(f"pb{i}", [128, 512], F32)

    def v(self, arena, off, dtype, cols, parts=128, p0=0):
        nb = cols * DT_BYTES[dtype]
        return arena.ap()[p0:p0 + parts, off:off + nb].bitcast(dtype)

    def psum(self, i, cols=512, dtype=F32):
        if dtype == F32:
            return self.ps[i].ap()[:, 0:cols]
        return self.ps[i].ap().bitcast(dtype)[:, 0:cols]

    def dbg(self, tag):
        if tag in self.debug:
            self.dbg_jobs.append(tag)

    def build(self):
        self.alloc()
        import contextlib
        stack = contextlib.ExitStack()
        for s in self.SEMS:
            self.sems[s] = stack.enter_context(self.nc.semaphore(s))
        with stack:
            with self.nc.Block() as block:
                for name, starter in (("sync", block.sync),
                                      ("gpsimd", block.gpsimd),
                                      ("scalar", block.scalar),
                                      ("vector", block.vector),
                                      ("tensor", block.tensor)):
                    def run(e, name=name):
                        self.cur = name
                        self.eng = e
                        self._reset()
                        self.schedule()
                    starter(run)

    def schedule(self):
        mp = self.max_phase
        self.ph_init()
        self._ph2_idx = 0
        self._ph2_state = {"map": self.ph2_map("ph1")}
        self._ph2_maps = {}
        self._ph2_barriers = {}
        self.ag_cc = {}
        self._ph2g = self.ph2_gen() if mp >= 2 else None
        self._ph0g = self.ph0_gen()
        with self.nc.named_scope("ph1"):
            if mp >= 1:
                self.ph1()
        with self.nc.named_scope("ph0"):
            self.ph0()
        assert self._ph2g is None or self._ph2_idx == self.PH2_N1 \
            or mp < 6, self._ph2_idx
        bar3 = [("st1", self.c["st1"]), ("ve", self.c["ve"])]
        self._ph2_maps[self.PH2_N1] = self.ph2_map("ph3")
        self._ph2_barriers[self.PH2_N1] = {
            eng: list(bar3) for eng in ("sync", "scalar", "vector")}
        self.mark_ph1 = dict(gp=self.c["gp"], sc=self.c["sc"],
                             w0=self.c["w0"], w1=self.c["w1"])
        with self.nc.named_scope("ph3"):
            if mp >= 3:
                self.ph3()
        assert self._ph2g is None or self._ph2_idx == self.PH2_N3 \
            or mp < 6, self._ph2_idx
        tebar = self.c["te"]
        self._ph2_maps[self.PH2_N3] = self.ph2_map("ph4")
        self._ph2_barriers[self.PH2_N3] = {
            eng: [("te", tebar)]
            for eng in ("sync", "scalar", "gpsimd", "vector")}
        with self.nc.named_scope("ph4"):
            if mp >= 4:
                self.ph4()
        assert self._ph2g is None or self._ph2_idx == self.PH2_N4 \
            or mp < 6, self._ph2_idx
        bar5 = [("te", self.c["te"]), ("ve", self.c["ve"]),
                ("as", self.c["as"])]
        self._ph2_maps[self.PH2_N4] = self.ph2_map("ph5")
        self._ph2_barriers[self.PH2_N4] = {
            eng: list(bar5) for eng in ("sync", "scalar", "vector")}
        self.mark_ph4 = dict(w0=self.c["w0"], w1=self.c["w1"])
        with self.nc.named_scope("ph5"):
            if mp >= 5:
                self.ph5()
        with self.nc.named_scope("ph6"):
            if mp >= 6:
                self.ph6()
        with self.nc.named_scope("ph2fin"):
            if mp >= 2:
                self.ph2_finish()
        self.fin()

    def pump(self, n):
        if self._ph2g is None:
            return
        for _ in range(n):
            try:
                next(self._ph2g)
            except StopIteration:
                self._ph2g = None
                return

    def ph_init(self):
        d = self.dram
        self.dma("sync", self.ident_sb.ap(), d["ident"], sem="c0")
        self.op("gpsimd", "memset", self.ones_bf.ap(), 1.0, sem="gp")
        self.op("gpsimd", "memset", self.consts.ap()[:, 0:1], float(EPS), sem="gp")
        self.op("gpsimd", "memset", self.consts.ap()[:, 1:2], float(EXP_SHIFT), sem="gp")

    def ph0_gen(self):
        d = self.dram
        jobs = []
        for kk in range(32):
            jobs.append(("wq_c", "wq_s", kk, 1,
                         self.v(self.B, kk * 1024, BF16, 512)))
        for kk0 in range(0, 32, 4):
            jobs.append(("wk_c", "wk_s", kk0, 4,
                         self.v(self.F, kk0 * 256, BF16, 512)))
        for kk0 in range(0, 32, 4):
            jobs.append(("wv_c", "wv_s", kk0, 4,
                         self.v(self.F, 8192 + kk0 * 256, BF16, 512)))
        st = {}
        self.ph0_ve_at = {}
        for idx, (cnm, snm, kk0, stk, dst) in enumerate(jobs):
            par = idx % 2
            cod = self.v(self.A, par * 512, U8, 512)
            scl = self.v(self.A, 1024 + par * 2048, F32, 512)
            aff = self.v(self.A, 5120 + par * 2048, F32, 512)
            src = d[cnm][kk0 * 128:(kk0 + stk) * 128, :]
            srcs = d[snm][kk0 * 128:(kk0 + stk) * 128, :]
            if stk > 1:
                src = src.rearrange("(a p) f -> p a f", p=128)
                srcs = srcs.rearrange("(a p) f -> p a f", p=128)
            if idx >= 2:
                self.wait("sync", "sc", st[("aff", idx - 2)])
                self.wait("sync", "ve", st[("mult", idx - 2)])
            cod3 = cod.rearrange("p (a f) -> p a f", a=stk) if stk > 1 else cod
            scl3 = scl.rearrange("p (a f) -> p a f", a=stk) if stk > 1 else scl
            self.dma("sync", cod3, src, sem=f"p{par}")
            self.dma("sync", scl3, srcs, sem=f"p{par}")
            ldv = self.c[f"p{par}"]
            self.wait("scalar", f"p{par}", ldv)
            if idx >= 2:
                self.wait("scalar", "ve", st[("mult", idx - 2)])
            scv = self.op("scalar", "activation", aff, cod, AF.Copy,
                          bias=-4.0, scale=0.5, sem="sc")
            st[("aff", idx)] = scv
            self.wait("vector", f"p{par}", ldv)
            self.wait("vector", "sc", scv)
            st[("mult", idx)] = self.op(
                "vector", "tensor_tensor", out=dst, in0=aff, in1=scl,
                op=ALU.mult, sem="ve")
            self.ph0_ve_at[cnm] = st[("mult", idx)]
            yield

    def pump0(self, n):
        if self._ph0g is None:
            return
        for _ in range(n):
            try:
                next(self._ph0g)
            except StopIteration:
                self._ph0g = None
                return

    def ph0(self):
        while self._ph0g is not None:
            self.pump0(1)
        self.ph0_ve = self.c["ve"]


    def _ln_tile(self, xt, sq, xln, st):
        self.op("vector", "reduce_sum", st[:, 0:1], xt, axis=AX.X, sem="ve")
        self.wait("vector", "ve", self.c["ve"])
        self.op("scalar", "activation", sq, xt, AF.Square,
                accum_out=st[:, 1:2], sem="sc")
        self.wait("vector", "sc", self.c["sc"])
        self.op("vector", "tensor_scalar_mul", st[:, 2:3], st[:, 0:1], 1.0 / DM)
        self.op("vector", "tensor_scalar_mul", st[:, 3:4], st[:, 1:2], 1.0 / DM)
        self.op("vector", "tensor_tensor", out=st[:, 4:5], in0=st[:, 2:3],
                in1=st[:, 2:3], op=ALU.mult, sem="ve")
        self.wait("vector", "ve", self.c["ve"])
        self.op("vector", "tensor_tensor", out=st[:, 5:6], in0=st[:, 3:4],
                in1=st[:, 4:5], op=ALU.subtract, sem="ve")
        self.wait("scalar", "ve", self.c["ve"])
        self.op("scalar", "activation", st[:, 6:7], st[:, 5:6], AF.Sqrt,
                bias=self.consts.ap()[:, 0:1], scale=1.0, sem="sc")
        self.wait("vector", "sc", self.c["sc"])
        self.op("vector", "reciprocal", st[:, 7:8], st[:, 6:7], sem="ve")
        self.wait("vector", "ve", self.c["ve"])
        self.op("vector", "tensor_tensor", out=st[:, 8:9], in0=st[:, 2:3],
                in1=st[:, 7:8], op=ALU.mult, sem="ve")
        self.wait("vector", "ve", self.c["ve"])
        self.op("vector", "tensor_scalar_mul", st[:, 9:10], st[:, 8:9], -1.0,
                sem="ve")
        self.wait("scalar", "ve", self.c["ve"])
        self.op("scalar", "activation", xln, xt, AF.Identity,
                bias=st[:, 9:10], scale=st[:, 7:8], sem="sc")

    def ph1(self):
        d = self.dram
        st = self.stats.ap()
        self.wait("scalar", "gp", 3)
        norm_sc = {}
        ln_te = {}
        tr_ve = {}
        prev_dma_st = 0
        for mt in range(8):
            par = mt % 2
            xt = self.v(self.C, par * 16384, F32, 4096)
            sq = self.v(self.Dr, 0, F32, 4096)
            xln = self.v(self.E, 0, BF16, 4096)
            fmst = self.v(self.E, 8192, BF16, 4096)

            if mt >= 2:
                self.wait("sync", "sc", norm_sc[mt - 2])
            self.dma("sync", xt, d["x_own"][mt * 128:(mt + 1) * 128, :],
                     sem=f"x{par}")
            ldv = self.c[f"x{par}"]
            self.wait("vector", f"x{par}", ldv)
            self.wait("scalar", f"x{par}", ldv)
            if mt >= 1:
                self.wait("scalar", "te", ln_te[mt - 1])
            self._ln_tile(xt, sq, xln, st)
            norm_sc[mt] = self.c["sc"]
            for grp in range(4):
                bank = 4 + (grp % 2)
                pbt = self.psum(bank, 1024, BF16)
                self.wait("tensor", "sc", norm_sc[mt])
                self.wait("tensor", "ve", tr_ve.get(bank, 0))
                for t8 in range(8):
                    kk = grp * 8 + t8
                    self.op("tensor", "transpose",
                            pbt[:, t8 * 128:(t8 + 1) * 128],
                            xln[:, kk * 128:(kk + 1) * 128],
                            self.ident_sb.ap(), sem="te" if t8 == 7 else None)
                tev = self.c["te"]
                self.wait("vector", "te", tev)
                if grp == 0:
                    self.wait("vector", "st1", prev_dma_st)
                tr_ve[bank] = self.op(
                    "vector", "tensor_copy",
                    fmst[:, grp * 1024:(grp + 1) * 1024], pbt, sem="ve")
            ln_te[mt] = self.c["te"]
            seng = "sync" if mt % 2 == 0 else "gpsimd"
            self.wait(seng, "ve", self.c["ve"])
            half = "a" if mt < 4 else "b"
            dst = d[f"xln_own_{half}"].rearrange(
                "p (kk t) -> p kk t", kk=32)[:, :,
                                             (mt % 4) * 128:(mt % 4 + 1) * 128]
            prev_dma_st = self.dma(
                seng, dst,
                fmst.rearrange("p (kk f) -> p kk f", kk=32), sem="st1")
            self.pump(2)
            if mt == 3:
                self.wait("gpsimd", "st1", self.c["st1"])
                self.coll("AllGather", [d["xln_own_a"]], [d["xln_all_a"]])
        self.wait("gpsimd", "st1", self.c["st1"])
        self.coll("AllGather", [d["xln_own_b"]], [d["xln_all_b"]])
        self.dbg("xln_own")
        self.dbg("xln")
        self.ph1_trve = dict(tr_ve)

    SC_T = (0, 1, 3)
    GP_T = (2, 4, 5, 6, 7)

    PH2_N1, PH2_N3, PH2_N4 = 8, 40, 72

    def ph2_width(self, n):
        return 512 if self.PH2_N1 <= n < self.PH2_N3 else 1024

    def ph2_map(self, phase):
        if phase == "ph1":
            base = [(self.A, 16384 + o) for o in
                    (0, 4096, 8192, 12288, 13312, 14336, 15360, 19456, 21504)]
            mp = dict(zip(("p0", "s0", "s1", "m", "cod0", "cod1", "scl",
                           "w0", "w1"), base))
            mp["p0b"] = (self.A, 16384 + 23552)
            return mp
        if phase == "ph3":
            base = [(self.E, 8192), (self.E, 10240), (self.E, 12288),
                    (self.E, 14336), (self.E, 14848), (self.E, 15360),
                    (self.F, 16384), (self.F, 18432), (self.F, 19456)]
            return dict(zip(("p0", "s0", "s1", "m", "cod0", "cod1", "scl",
                             "w0", "w1"), base))
        if phase == "ph4":
            base = [(self.C, o) for o in
                    (0, 4096, 8192, 12288, 13312, 14336, 15360, 19456, 21504)]
            mp = dict(zip(("p0", "s0", "s1", "m", "cod0", "cod1", "scl",
                           "w0", "w1"), base))
            mp["p0b"] = (self.C, 23552)
            return mp
        base = [(self.E, 0), (self.E, 4096), (self.E, 8192),
                (self.E, 12288), (self.E, 13312), (self.E, 14336),
                (self.G, 0), (self.G, 4096), (self.G, 6144)]
        return dict(zip(("p0", "s0", "s1", "m", "cod0", "cod1", "scl",
                         "w0", "w1"), base))

    def _ph2_chunk_jobs(self, k, n0):
        cwk = CHUNK_COLS[k]
        jobs = []
        n = n0
        for cnm, snm, onm in (("g_c", "g_s", f"gw_own{k}"),
                              ("u_c", "u_s", f"uw_own{k}")):
            rt0 = 0
            while rt0 < 32:
                stk = self.ph2_width(n) // cwk
                jobs.append(("gu", cnm, snm, onm, rt0, stk, 256 * k, cwk, 0))
                rt0 += stk
                n += 1
        for rt in range(cwk // 128):
            ci = 0
            while ci < 4096:
                w = self.ph2_width(n)
                jobs.append(("dw", "d_c", "d_s", f"dw_own{k}", 2 * k + rt,
                             1, ci, w, rt))
                ci += w
                n += 1
        return jobs

    def ph2_gen(self):
        d = self.dram
        n = 0
        for k in range(6):
            jobs = self._ph2_chunk_jobs(k, n)
            n += len(jobs)
            for job in jobs:
                yield from self._ph2_job(job)
            self.wait("gpsimd", "w0", self.c["w0"])
            self.wait("gpsimd", "w1", self.c["w1"])
            self.coll("AllGather", [d[f"gw_own{k}"]], [d[f"gw_all{k}"]])
            self.coll("AllGather", [d[f"uw_own{k}"]], [d[f"uw_all{k}"]])
            self.coll("AllGather", [d[f"dw_own{k}"]], [d[f"dw_all{k}"]])
            self.ag_cc[k] = self.c["cc"]

    def _ph2_job(self, job):
        d = self.dram
        kind, cnm, snm, onm, rt0, stk, c0, cw, rtl = job
        idx = self._ph2_idx
        self._ph2_idx += 1
        st = self._ph2_state
        if idx in self._ph2_maps:
            st["map"] = self._ph2_maps[idx]
        mp = st["map"]
        bar = self._ph2_barriers.get(idx)
        if bar:
            for eng, lst in bar.items():
                for semn, val in lst:
                    self.wait(eng, semn, val)
        par = idx % 2
        W1 = stk * cw
        dbp = "p0b" in mp
        p0key = "p0b" if (dbp and par) else "p0"
        p0 = self.v(mp[p0key][0], mp[p0key][1], F32, W1)
        slot = [self.v(mp["s0"][0], mp["s0"][1], F32, W1),
                self.v(mp["s1"][0], mp["s1"][1], F32, W1)]
        m = self.v(mp["m"][0], mp["m"][1], U8, W1)
        cod = self.v(mp[f"cod{par}"][0], mp[f"cod{par}"][1], U8, W1)
        scl = self.v(mp["scl"][0], mp["scl"][1], F32, W1)
        wout = self.v(mp[f"w{par}"][0], mp[f"w{par}"][1], BF16, W1)

        rsl = slice(rt0 * 128, (rt0 + stk) * 128)
        csl = slice(c0, c0 + cw)

        ve0 = self.c["ve"]
        for t in range(1, 8):
            st[("cp", idx, t)] = ve0 + t

        def src3(nm):
            s = d[nm][rsl, csl]
            if stk == 1:
                return s
            return s.rearrange("(a p) f -> p a f", p=128)

        def sb3(view):
            if stk == 1:
                return view
            return view.rearrange("p (a f) -> p a f", a=stk)

        if idx >= 2:
            self.wait("sync", "ve", st[("cp", idx - 2, 7)])
            self.wait("sync", "sc", st[("aff_sc", idx - 2, 7)])
        self.dma("sync", sb3(cod), src3(cnm), sem=f"p{par}")
        st[("cod_ld", idx)] = self.c[f"p{par}"]
        if idx >= 1:
            self.wait("sync", "gp", st[("mult", idx - 1)])
        self.dma("sync", sb3(scl), src3(snm), sem="p2")
        st[("scl_ld", idx)] = self.c["p2"]

        def coef(t):
            dt_ = float(NF4[2 * t + 1] - NF4[2 * t])
            return dt_, float(NF4[2 * t]) - dt_ * (2 * t)

        slot_of = lambda t: slot[(t - 1) % 2]
        def slot_free(eng, t):
            if t >= 3:
                self.wait(eng, "ve", st[("cp", idx, t - 2)])
            elif idx >= 1:
                self.wait(eng, "ve", st[("cp", idx - 1, 8 - t)])

        self.wait("scalar", f"p{par}", st[("cod_ld", idx)])
        pback = 2 if dbp else 1
        if idx >= pback:
            self.wait("scalar", "gp", st[("mult", idx - pback)])
        self.wait("vector", f"p{par}", st[("cod_ld", idx)])
        dt_, bt = coef(0)
        self.op("scalar", "activation", p0, cod, AF.Copy, bias=bt, scale=dt_,
                sem="sc")
        st[("aff_sc", idx, 0)] = self.c["sc"]
        for t in range(1, 8):
            slot_free("scalar", t)
            dt_, bt = coef(t)
            self.op("scalar", "activation", slot_of(t), cod, AF.Copy,
                    bias=bt, scale=dt_, sem="sc")
            st[("aff_sc", idx, t)] = self.c["sc"]
            self.op("vector", "tensor_single_scalar", out=m, in_=cod,
                    scalar=float(2 * t), op=ALU.is_ge)
            self.wait("vector", "sc", st[("aff_sc", idx, t)])
            self.op("vector", "copy_predicated", p0, m, slot_of(t), sem="ve")
            if t == 3:
                yield
        self.wait("gpsimd", "ve", st[("cp", idx, 7)])
        self.wait("gpsimd", "p2", st[("scl_ld", idx)])
        if idx >= 2:
            self.wait("gpsimd", f"w{par}", st[("store", idx - 2)])
        self.op("gpsimd", "tensor_tensor", out=wout, in0=p0, in1=scl,
                op=ALU.mult, sem="gp")
        st[("mult", idx)] = self.c["gp"]
        self.wait("gpsimd", "gp", st[("mult", idx)])
        if kind == "gu":
            dlk = cw // 128
            for dlb in range(dlk):
                dst3 = d[onm].rearrange(
                    "p (dl kk f) -> p dl kk f", dl=dlk, kk=32)[
                    :, dlb:dlb + 1, rt0:rt0 + stk, :]
                srcw = wout.rearrange(
                    "p (a dl f) -> p dl a f", a=stk, dl=dlk, f=128)[
                    :, dlb:dlb + 1, :, :]
                self.dma("gpsimd", dst3, srcw, sem=f"w{par}")
        else:
            nrt = d[onm].shape[1] // (8 * 512)
            dst3 = d[onm].rearrange("p (hc rt f) -> p hc rt f",
                                    hc=8, rt=nrt)[:, c0 // 512:
                                                  c0 // 512 + cw // 512,
                                                  rtl:rtl + 1, :]
            srcw = wout.rearrange("p (hc u f) -> p hc u f",
                                  hc=cw // 512, u=1)
            self.dma("gpsimd", dst3, srcw, sem=f"w{par}")
        st[("store", idx)] = self.c[f"w{par}"]
        yield

    def ph2_finish(self):
        while self._ph2g is not None:
            self.pump(1)
        self.ph2_done_ve = self.c["ve"]

    def ph3(self):
        d = self.dram
        self.wait("sync", "cc", 1)
        self.wait("gpsimd", "cc", 1)
        self.wait("tensor", "ve", self.ph0_ve_at["wq_c"])
        self.wait("vector", "gp", self.mark_ph1["gp"])
        self.wait("vector", "sc", self.mark_ph1["sc"])
        self.wait("vector", "w0", self.mark_ph1["w0"])
        self.wait("vector", "w1", self.mark_ph1["w1"])
        qsb = self.v(self.A, 0, BF16, HPC * S)
        ksb = self.v(self.Dr, 0, BF16, S)
        vsb = self.v(self.Dr, 16384, BF16, S)
        cs = self.v(self.G, 0, F32, 2048, parts=64)
        prev_chunk_te = 0
        prev_rope_ve = 0
        bank_free_ve = {4: self.ph1_trve.get(4, 0), 5: self.ph1_trve.get(5, 0)}
        bank_free_sc = {}
        for j in list(range(0, 16, 2)) + list(range(1, 16, 2)):
            self.pump(2)
            if j == 1:
                self.wait("sync", "cc", 2)
            r = j // 2
            xch = self.v(self.C, 0, BF16, 32 * 512)
            self.wait("sync", "te", prev_chunk_te)
            src = d["xln_all_a" if j % 2 == 0 else "xln_all_b"][
                r * 128:(r + 1) * 128, :]
            self.dma("sync", xch, src, sem="c0")
            self.wait("sync", "ve", prev_rope_ve)
            self.dma("sync", cs, d["cs_tab"][:, j * 2048:(j + 1) * 2048],
                     sem="c0")
            ldv = self.c["c0"]
            self.wait("tensor", "c0", ldv)

            for nt in range(4):
                bank = nt % 2
                pq = self.psum(bank)
                self.wait("tensor", "ve", bank_free_ve.get(bank, 0))
                for kk in range(32):
                    self.op("tensor", "matmul", pq,
                            self.v(self.B, kk * 1024 + nt * 256, BF16, 128),
                            xch[:, kk * 512:(kk + 1) * 512],
                            start=(kk == 0), stop=(kk == 31),
                            sem="te" if kk == 31 else None)
                self.wait("vector", "te", self.c["te"])
                qdst = qsb[:, nt * S + j * 512: nt * S + j * 512 + 512]
                bank_free_ve[bank] = self._rope(pq, cs, qdst, scaled=True)
            pk = self.psum(2)
            self.wait("tensor", "ve", bank_free_ve.get(2, 0))
            if j == 0:
                self.wait("tensor", "ve", self.ph0_ve_at["wk_c"])
            for kk in range(32):
                self.op("tensor", "matmul", pk,
                        self.v(self.F, kk * 256, BF16, 128),
                        xch[:, kk * 512:(kk + 1) * 512],
                        start=(kk == 0), stop=(kk == 31),
                        sem="te" if kk == 31 else None)
            self.wait("vector", "te", self.c["te"])
            kdst = ksb[:, j * 512:(j + 1) * 512]
            bank_free_ve[2] = self._rope(pk, cs, kdst, scaled=False)
            prev_rope_ve = bank_free_ve[2]
            self.pump(2)
            for pt in range(4):
                bank = 3 + pt % 2
                pv = self.psum(bank, 128)
                self.wait("tensor", "sc", bank_free_sc.get(bank, 0))
                if j == 0 and pt == 0:
                    self.wait("tensor", "ve", self.ph0_ve_at["wv_c"])
                if bank == 4:
                    self.wait("tensor", "ve", bank_free_ve.get(4, 0))
                for kk in range(32):
                    self.op("tensor", "matmul", pv,
                            xch[:, kk * 512 + pt * 128: kk * 512 + (pt + 1) * 128],
                            self.v(self.F, 8192 + kk * 256, BF16, 128),
                            start=(kk == 0), stop=(kk == 31),
                            sem="te" if kk == 31 else None)
                self.wait("scalar", "te", self.c["te"])
                gt = j * 4 + pt
                bank_free_sc[bank] = self.op(
                    "scalar", "copy", vsb[:, gt * 128:(gt + 1) * 128], pv,
                    sem="sc")
                bank_free_ve[4] = bank_free_ve.get(4, 0)
            prev_chunk_te = self.c["te"]
        self.ph3_ve = self.c["ve"]
        self.ph3_sc = self.c["sc"]

    def _rope(self, pq, cs, dst, scaled):
        co = 0 if scaled else 1024
        cosv = cs[0:64, co:co + 512]
        sinv = cs[0:64, co + 512:co + 1024]
        t1 = self.v(self.E, 0, F32, 512, parts=64)
        t2 = self.v(self.E, 2048, F32, 512, parts=64)
        t3 = self.v(self.E, 4096, F32, 512, parts=64)
        t4 = self.v(self.E, 6144, F32, 512, parts=64)
        p1 = pq[0:64, :]
        p2 = pq[64:128, :]
        self.op("vector", "tensor_tensor", out=t1, in0=p1, in1=cosv, op=ALU.mult)
        self.op("vector", "tensor_tensor", out=t2, in0=p2, in1=sinv, op=ALU.mult)
        self.op("vector", "tensor_tensor", out=t3, in0=p2, in1=cosv, op=ALU.mult)
        self.op("vector", "tensor_tensor", out=t4, in0=p1, in1=sinv, op=ALU.mult)
        self.op("vector", "tensor_tensor", out=dst[0:64, :], in0=t1, in1=t2,
                op=ALU.subtract)
        return self.op("vector", "tensor_tensor", out=dst[64:128, :], in0=t3,
                       in1=t4, op=ALU.add, sem="ve")

    def ph4(self):
        d = self.dram
        self.ph4_tebar = self.c["te"]
        qsb = self.v(self.A, 0, BF16, HPC * S)
        ksb = self.v(self.Dr, 0, BF16, S)
        vsb = self.v(self.Dr, 16384, BF16, S)
        msk = self.v(self.G, 0, F32, 2048)
        self.wait("sync", "ve", self.ph3_ve)
        self.dma("sync", msk, d["masks"], sem="c0")
        ldm = self.c["c0"]
        self.wait("tensor", "c0", ldm)
        self.wait("tensor", "ve", self.ph3_ve)
        self.wait("tensor", "sc", self.ph3_sc)
        self.wait("scalar", "ve", self.ph3_ve)

        exp_done = {}
        pt_free = {}
        norm_ve = 0
        store_st = 0
        self.pre_deq_ve = {}
        pre_mult = {}
        woc = self.v(self.B, 0, BF16, 32 * 512)

        def pre_deq(kk):
            slot = kk % 3
            wsem = ("p3", "p4", "p5")[slot]
            cod = self.v(self.F, slot * 512, U8, 512)
            scl = self.v(self.F, 1536 + slot * 2048, F32, 512)
            aff = self.v(self.F, 7680 + slot * 2048, F32, 512)
            if kk < 3:
                self.wait("sync", "te", self.ph4_tebar)
            else:
                self.wait("sync", "ve", pre_mult[kk - 3])
            self.dma("sync", cod, d["wo_c"][kk * 128:(kk + 1) * 128, 0:512],
                     sem=wsem)
            self.dma("sync", scl, d["wo_s"][kk * 128:(kk + 1) * 128, 0:512],
                     sem=wsem)
            ldv = self.c[wsem]
            self.wait("scalar", wsem, ldv)
            if kk >= 3:
                self.wait("scalar", "ve", pre_mult[kk - 3])
            scv = self.op("scalar", "activation", aff, cod, AF.Copy,
                          bias=-4.0, scale=0.5, sem="sc")
            self.wait("vector", wsem, ldv)
            self.wait("vector", "sc", scv)
            if kk == 0:
                self.wait("vector", "te", self.ph4_tebar)
            pre_mult[kk] = self.op(
                "vector", "tensor_tensor", out=woc[:, kk * 512:(kk + 1) * 512],
                in0=aff, in1=scl, op=ALU.mult, sem="ve")
            self.pre_deq_ve[kk] = pre_mult[kk]

        for pair in range(2):
            for j in range(16):
                self.pump(1)
                pre_deq(pair * 16 + j)
                nkv = 4 * (j + 1)
                sc_of = {}
                te_s_of = {}
                for idx in range(nkv + 1):
                    if idx < nkv:
                        par = idx % 2
                        for hl in range(2):
                            bank = 2 * hl + par
                            pss = self.psum(bank)
                            self.wait("tensor", "sc",
                                      exp_done.get((hl, par), 0))
                            if idx == 0 and hl == 0:
                                self.wait("tensor", "ve", norm_ve)
                            h = pair * 2 + hl
                            te_s_of[(idx, hl)] = self.op(
                                "tensor", "matmul", pss,
                                ksb[:, idx * 128:(idx + 1) * 128],
                                qsb[:, h * S + j * 512: h * S + j * 512 + 512],
                                start=True, stop=True, sem="te")
                    if idx >= 1:
                        t = idx - 1
                        par = t % 2
                        diag = (t // 4 == j)
                        for hl in range(2):
                            bank = 2 * hl + par
                            pss = self.psum(bank)
                            ptv = self.v(self.E, (2 * hl + par) * 1024, BF16, 512)
                            tev = te_s_of[(t, hl)]
                            if diag:
                                t_in = t % 4
                                self.wait("vector", "te", tev)
                                vev = self.op(
                                    "vector", "tensor_tensor", out=pss, in0=pss,
                                    in1=msk[:, t_in * 512:(t_in + 1) * 512],
                                    op=ALU.add, sem="ve")
                                self.wait("scalar", "ve", vev)
                            else:
                                self.wait("scalar", "te", tev)
                            self.wait("scalar", "te", pt_free.get((hl, par), 0))
                            sc_of[(t, hl)] = self.op(
                                "scalar", "activation", ptv, pss, AF.Exp,
                                bias=self.consts.ap()[:, 1:2], scale=1.0,
                                sem="sc")
                            exp_done[(hl, par)] = sc_of[(t, hl)]
                            self.wait("tensor", "sc", sc_of[(t, hl)])
                            self.op("tensor", "matmul", self.psum(4 + hl),
                                    vsb[:, t * 128:(t + 1) * 128], ptv,
                                    start=(t == 0), stop=(t == nkv - 1),
                                    skip_group_check=True)
                            pt_free[(hl, par)] = self.op(
                                "tensor", "matmul", self.psum(6 + hl),
                                self.ones_bf.ap(), ptv,
                                start=(t == 0), stop=(t == nkv - 1),
                                skip_group_check=True, sem="te")
                self.wait("vector", "te", self.c["te"])
                self.wait("vector", "as", store_st)
                for hl in range(2):
                    rec = self.v(self.E, 4096, F32, 512)
                    astg = self.v(self.E, 6144 + hl * 1024, BF16, 512)
                    self.op("vector", "reciprocal", rec, self.psum(6 + hl),
                            sem="ve")
                    self.wait("vector", "ve", self.c["ve"])
                    self.op("vector", "tensor_tensor", out=astg,
                            in0=self.psum(4 + hl), in1=rec, op=ALU.mult,
                            sem="ve")
                norm_ve = self.c["ve"]
                self.wait("sync", "ve", norm_ve)
                astg2 = self.v(self.E, 6144, BF16, 1024)
                r, off = j // 2, (j % 2) * 512
                dst = d[f"attn_a2a_p{pair}"][r * 128:(r + 1) * 128, :]\
                    .rearrange("p (kl t) -> p kl t", kl=2)[:, :, off:off + 512]
                store_st = self.dma(
                    "sync", dst,
                    astg2.rearrange("p (h f) -> p h f", h=2), sem="as")
                self.pump(1)
            self.wait("gpsimd", "as", self.c["as"])
            self.coll("AllToAll", [d[f"attn_a2a_p{pair}"]],
                      [d[f"attn_loc_p{pair}"]])
            if pair == 0:
                self.a2a_cc_a = self.c["cc"]
        self.a2a_cc = self.c["cc"]
        self.dbg("a2a_in")
        self.dbg("attn")
        self.ph4_norm_ve = norm_ve

    def ph5(self):
        d = self.dram
        self.wait("sync", "cc", self.a2a_cc_a)
        self.wait("sync", "ve", self.c["ve"])
        self.wait("sync", "gp", self.c["gp"])
        self.wait("sync", "w0", self.c["w0"])
        self.wait("sync", "w1", self.c["w1"])
        for cslab in range(8):
            self.dma("sync", self.v(self.A, cslab * 8192, BF16, 2048),
                     d["attn_loc_p0"][cslab * 128:(cslab + 1) * 128, :],
                     sem="c0")
        slab_ld_a = self.c["c0"]
        self.wait("sync", "cc", self.a2a_cc)
        for cslab in range(8):
            self.dma("sync", self.v(self.A, cslab * 8192 + 4096, BF16, 2048),
                     d["attn_loc_p1"][cslab * 128:(cslab + 1) * 128, :],
                     sem="c0")
        slab_ld = self.c["c0"]

        def slab_ap(kk, mt):
            return self.v(self.A, (kk // 4) * 8192 + (kk % 4) * 2048
                          + mt * 256, BF16, 128)

        woc = self.v(self.B, 0, BF16, 32 * 512)
        deq_ve = {}
        woc_te = {}
        oproj_te = 0
        ostore_ad = 0
        prev_evac_ve = self.ph4_norm_ve
        WOSEM = ("p3", "p4", "p5", "g0")
        KK_ORDER = ([4 * cc + kl for cc in range(8) for kl in (0, 1)] +
                    [4 * cc + kl for cc in range(8) for kl in (2, 3)])
        for ncb in range(8):
            for pos, kk in enumerate(KK_ORDER):
                if ncb == 0:
                    if pos == 0:
                        self.wait("tensor", "c0", slab_ld_a)
                        self.wait("tensor", "ve", prev_evac_ve)
                    if pos == 16:
                        self.wait("tensor", "c0", slab_ld)
                    self.wait("tensor", "ve", self.pre_deq_ve[kk])
                else:
                    slot = pos % 4
                    wsem = WOSEM[slot]
                    cod = self.v(self.C, slot * 512, U8, 512)
                    scl = self.v(self.C, 2048 + slot * 2048, F32, 512)
                    aff = self.v(self.C, 10240 + slot * 2048, F32, 512)
                    self.wait("sync", "ve", deq_ve.get(slot, 0))
                    csl = slice(ncb * 512, (ncb + 1) * 512)
                    rsl = slice(kk * 128, (kk + 1) * 128)
                    self.dma("sync", cod, d["wo_c"][rsl, csl], sem=wsem)
                    self.dma("sync", scl, d["wo_s"][rsl, csl], sem=wsem)
                    ldv = self.c[wsem]
                    self.wait("scalar", wsem, ldv)
                    self.wait("scalar", "ve", deq_ve.get(slot, 0))
                    scv = self.op("scalar", "activation", aff, cod, AF.Copy,
                                  bias=-4.0, scale=0.5, sem="sc")
                    self.wait("vector", "sc", scv)
                    self.wait("vector", "te", woc_te.get(kk, 0))
                    deq_ve[slot] = self.op(
                        "vector", "tensor_tensor",
                        out=woc[:, kk * 512:(kk + 1) * 512],
                        in0=aff, in1=scl, op=ALU.mult, sem="ve")
                    if pos == 0:
                        self.wait("tensor", "ve", prev_evac_ve)
                    self.wait("tensor", "ve", deq_ve[slot])
                for mt in range(8):
                    self.op("tensor", "matmul", self.psum(mt),
                            slab_ap(kk, mt),
                            woc[:, kk * 512:(kk + 1) * 512],
                            start=(pos == 0), stop=(pos == 31),
                            skip_group_check=True,
                            sem="te" if mt == 7 else None)
                woc_te[kk] = self.c["te"]
            oproj_te = self.c["te"]
            xcol = self.v(self.Dr, 0, F32, 8 * 512)
            src = d["x_own"][:, ncb * 512:(ncb + 1) * 512]
            self.wait("sync", "ve", prev_evac_ve)
            self.dma("sync", xcol.rearrange("p (mt f) -> p mt f", mt=8),
                     src.rearrange("(mt p) f -> p mt f", p=128), sem="c0")
            xldv = self.c["c0"]
            ostg = self.v(self.Dr, 16384, F32, 8 * 512)
            self.wait("vector", "te", oproj_te)
            self.wait("vector", "c0", xldv)
            self.wait("vector", "ad", ostore_ad)
            for mt in range(8):
                self.op("vector", "tensor_tensor",
                        out=ostg[:, mt * 512:(mt + 1) * 512],
                        in0=self.psum(mt), in1=xcol[:, mt * 512:(mt + 1) * 512],
                        op=ALU.add, sem="ve" if mt == 7 else None)
            prev_evac_ve = self.c["ve"]
            self.wait("gpsimd", "ve", prev_evac_ve)
            dst = d["out"][:, ncb * 512:(ncb + 1) * 512]
            ostore_ad = self.dma(
                "gpsimd", dst.rearrange("(mt p) f -> p mt f", p=128),
                ostg.rearrange("p (mt f) -> p mt f", mt=8), sem="ad")
        self.ph5_evac_ve = prev_evac_ve

        st = self.stats.ap()
        self.wait("sync", "ad", self.c["ad"])
        ln2_sc = {}
        ln_te = {}
        tr_ve = {4: prev_evac_ve, 5: prev_evac_ve}
        for mt in range(8):
            self.pump(2)
            par = mt % 2
            xt = self.v(self.C, par * 16384, F32, 4096)
            sq = self.v(self.B, 0, F32, 4096)
            xln = self.v(self.F, par * 8192, BF16, 4096)
            if mt >= 2:
                self.wait("sync", "sc", ln2_sc[mt - 2])
            self.dma("sync", xt, d["out"][mt * 128:(mt + 1) * 128, :],
                     sem=f"x{par}")
            ldv = self.c[f"x{par}"]
            self.wait("vector", f"x{par}", ldv)
            self.wait("scalar", f"x{par}", ldv)
            if mt >= 2:
                self.wait("scalar", "te", ln_te[mt - 2])
            self._ln_tile(xt, sq, xln, st)
            ln2_sc[mt] = self.c["sc"]
            for grp in range(4):
                bank = 4 + (grp % 2)
                pbt = self.psum(bank, 1024, BF16)
                self.wait("tensor", "sc", ln2_sc[mt])
                self.wait("tensor", "ve", tr_ve.get(bank, 0))
                for t8 in range(8):
                    kk = grp * 8 + t8
                    self.op("tensor", "transpose",
                            pbt[:, t8 * 128:(t8 + 1) * 128],
                            xln[:, kk * 128:(kk + 1) * 128],
                            self.ident_sb.ap(), sem="te" if t8 == 7 else None)
                tev = self.c["te"]
                self.wait("vector", "te", tev)
                for t8 in range(8):
                    kk = grp * 8 + t8
                    self.op("vector", "tensor_copy",
                            self.v(self.A, kk * 2048 + mt * 256, BF16, 128),
                            pbt[:, t8 * 128:(t8 + 1) * 128],
                            sem="ve" if t8 == 7 else None)
                tr_ve[bank] = self.c["ve"]
            ln_te[mt] = self.c["te"]
        self.ph5_ln2_ve = self.c["ve"]
        self.mark_ph5 = dict(sc=self.c["sc"], ve=self.c["ve"],
                             te=self.c["te"], ad=self.c["ad"])
        self.dbg("x2")

    def ph6(self):
        d = self.dram
        self.wait("scalar", "w0", self.mark_ph4["w0"])
        self.wait("scalar", "w1", self.mark_ph4["w1"])
        xln2 = lambda kk, mc: self.v(self.A, kk * 2048 + mc * 1024, BF16, 512)
        inter = self.v(self.B, 0, BF16, 16 * 1024)
        gu_te = {}
        inter_ve = {0: self.ph5_ln2_ve, 1: self.ph5_ln2_ve}
        dsl_te = {}
        evac_ve = 0
        evac_half = {}
        half_te = {}
        add_ad = 0
        down_te_last = 0
        for sbi in range(6):
            tiles = SB_TILES[sbi]
            nsb = len(tiles)
            while sbi not in self.ag_cc and self._ph2g is not None:
                self.pump(1)
            ccv = self.ag_cc[sbi]
            for il, (r, dl) in enumerate(tiles):
                par = il % 2
                gsl = self.v(self.C, par * 8192, BF16, 32 * 128)
                usl = self.v(self.C, 16384 + par * 8192, BF16, 32 * 128)
                self.wait("sync", "cc", ccv)
                self.wait("sync", "te", gu_te.get(par, 0))
                if sbi == 0 and il <= 1:
                    self.wait("sync", "sc", self.mark_ph5["sc"])
                    self.wait("sync", "ve", self.mark_ph5["ve"])
                gsrc = d[f"gw_all{sbi}"][r * 128:(r + 1) * 128,
                                         dl * DM:(dl + 1) * DM]
                usrc = d[f"uw_all{sbi}"][r * 128:(r + 1) * 128,
                                         dl * DM:(dl + 1) * DM]
                self.dma("sync", gsl, gsrc, sem=f"g{par}")
                self.dma("sync", usl, usrc, sem=f"g{par}")
                ldv = self.c[f"g{par}"]
                bb = 4 * par
                self.pump(4)
                self.wait("tensor", f"g{par}", ldv)
                self.wait("tensor", "ve", inter_ve.get(par, 0))
                if il <= 1:
                    self.wait("tensor", "ve", evac_ve)
                    if sbi == 0:
                        self.wait("tensor", "ve", self.ph5_ln2_ve)
                for kk in range(32):
                    for mc in range(2):
                        self.op("tensor", "matmul", self.psum(bb + mc),
                                gsl[:, kk * 128:(kk + 1) * 128],
                                xln2(kk, mc), start=(kk == 0), stop=(kk == 31),
                                skip_group_check=True)
                for kk in range(32):
                    for mc in range(2):
                        self.op("tensor", "matmul", self.psum(bb + 2 + mc),
                                usl[:, kk * 128:(kk + 1) * 128],
                                xln2(kk, mc), start=(kk == 0), stop=(kk == 31),
                                skip_group_check=True,
                                sem="te" if kk == 31 and mc == 1 else None)
                gu_te[par] = self.c["te"]
                for mc in range(2):
                    sstg = self.v(self.F, 16384 + (2 * par + mc) * 1024, BF16,
                                  512)
                    self.wait("scalar", "te", gu_te[par])
                    scv = self.op("scalar", "activation", sstg,
                                  self.psum(bb + mc),
                                  AF.Silu, bias=0.0, scale=1.0, sem="sc")
                    self.wait("vector", "sc", scv)
                    if sbi > 0 and il == 0 and mc == 0:
                        self.wait("vector", "te", down_te_last)
                    self.op("vector", "tensor_tensor",
                            out=inter[:, il * 1024 + mc * 512:
                                      il * 1024 + (mc + 1) * 512],
                            in0=self.psum(bb + 2 + mc), in1=sstg, op=ALU.mult,
                            sem="ve")
                inter_ve[par] = self.c["ve"]
            allint_ve = self.c["ve"]
            nrt = CHUNK_COLS[sbi] // 128
            for hc in range(8):
                par = hc % 2
                dsl = self.v(self.Dr, par * 16384, BF16, nsb * 512)
                self.wait("sync", "te", dsl_te.get(par, 0))
                if sbi == 0 and hc <= 1:
                    self.wait("sync", "te", self.mark_ph5["te"])
                    self.wait("sync", "ve", self.mark_ph5["ve"])
                    self.wait("sync", "ad", self.mark_ph5["ad"])
                dsem = f"x{par}"
                dsrc = d[f"dw_all{sbi}"].rearrange(
                    "(r p) (hc c) -> p r hc c", p=128, hc=8)[:, :,
                                                             hc:hc + 1, :]
                self.dma("sync",
                         self.v(self.Dr, par * 16384, BF16, 8 * nrt * 512)
                         .rearrange("p (r u c) -> p r u c", r=8, u=1),
                         dsrc, sem=dsem)
                ldv = self.c[dsem]
                self.pump(4)
                self.wait("tensor", dsem, ldv)
                self.wait("tensor", "ve", allint_ve)
                dstg = self.v(self.F, 0, F32, 8 * 512)
                for half in range(2):
                    mts = range(4 * half, 4 * half + 4)
                    self.wait("tensor", "ve", evac_half.get(half, 0))
                    for il, (r, dl) in enumerate(tiles):
                        iti = r * nrt + dl
                        for mt in mts:
                            self.op("tensor", "matmul", self.psum(mt),
                                    inter[:, il * 1024 + mt * 128:
                                          il * 1024 + mt * 128 + 128],
                                    dsl[:, iti * 512:(iti + 1) * 512],
                                    start=(il == 0), stop=(il == nsb - 1),
                                    skip_group_check=True,
                                    sem="te" if il == nsb - 1 and mt == mts[-1]
                                    else None)
                    half_te[half] = self.c["te"]
                    self.wait("vector", "te", half_te[half])
                    if half == 0:
                        self.wait("vector", "ad", add_ad)
                    for mt in mts:
                        self.op("vector", "tensor_copy",
                                dstg[:, mt * 512:(mt + 1) * 512], self.psum(mt),
                                sem="ve" if mt == mts[-1] else None)
                    evac_half[half] = self.c["ve"]
                dsl_te[par] = self.c["te"]
                down_te_last = self.c["te"]
                evac_ve = self.c["ve"]
                self.wait("gpsimd", "ve", evac_ve)
                dstd = d["out"][:, hc * 512:(hc + 1) * 512]
                add_ad = self.dma(
                    "gpsimd", dstd.rearrange("(mt p) f -> p mt f", p=128),
                    dstg.rearrange("p (mt f) -> p mt f", mt=8), sem="ad",
                    accum=ALU.add)

    def fin(self):
        if "wq" in self.debug:
            self.wait("sync", "ve", self.c["ve"])
            self.dma("sync", self.dram["dbg_wq"],
                     self.v(self.B, 0, BF16, 32 * 512), sem="st1")
        if "qkv" in self.debug and self.max_phase in (3, 4):
            self.wait("sync", "ve", self.c["ve"])
            self.wait("sync", "sc", self.c["sc"])
            self.wait("sync", "te", self.c["te"])
            self.dma("sync", self.dram["dbg_q"],
                     self.v(self.A, 0, BF16, HPC * S), sem="st1")
            self.dma("sync", self.dram["dbg_k"],
                     self.v(self.Dr, 0, BF16, S), sem="st1")
            self.dma("sync", self.dram["dbg_v"],
                     self.v(self.Dr, 16384, BF16, S), sem="st1")
        for tag in self.dbg_jobs:
            for sm in ("st1", "as", "w0", "w1", "ad", "cc"):
                self.wait("sync", sm, self.c[sm])
            self.dma("sync", self.dram["dbg_" + tag],
                     self.dram[DBG_TAGS[tag]], sem="st1")
        self.wait("gpsimd", "ad", self.c["ad"])
        self.wait("sync", "st1", self.c["st1"])
        self.wait("sync", "as", self.c["as"])
        self.wait("sync", "ad", self.c["ad"])



def build_nc(max_phase=6, debug=()):
    nc = bass.Bass("TRN2", target_bir_lowering=False)
    b = Builder(nc, max_phase=max_phase, debug=debug)
    b.build()
    return nc, b


def kernel(**inputs):
    in_maps = host_prep(inputs)
    nc, b = build_nc()
    res = run_bass_kernel_spmd(nc, in_maps, core_ids=list(range(W)))
    out = np.concatenate([np.asarray(r["out"]) for r in res.results], axis=0)
    return out.reshape(1, S, DM).astype(np.float32)
